# version 28
# baseline (speedup 1.0000x reference)
import contextlib

import numpy as np
import ml_dtypes
import concourse.bass as bass
import concourse.mybir as mybir
from concourse.bass_utils import run_bass_kernel_spmd

F32 = mybir.dt.float32
BF16 = mybir.dt.bfloat16
I16 = mybir.dt.int16
U16 = mybir.dt.uint16
AF = mybir.ActivationFunctionType
ALU = mybir.AluOpType
AX = mybir.AxisListType
BF = ml_dtypes.bfloat16

ENGINES = ("tensor", "vector", "scalar", "gpsimd", "sync")

DM = 1024
H = 16
DH = 64
CH = 128
NI = 3072
NIC = NI // 128
QMAX = 32767.0
LN_EPS = 1e-5
RMS_EPS = float(np.finfo(np.float32).eps)


class Buf:
    __slots__ = ("name", "writer", "readers")

    def __init__(self, name):
        self.name = name
        self.writer = None
        self.readers = []


class Prog:

    def __init__(self, nc):
        self.nc = nc
        self.stack = contextlib.ExitStack()
        self.items = {e: [] for e in ENGINES}
        self.ccount = {e: 0 for e in ENGINES}
        self.dcount = {}
        self.csem = {}
        self.dsem = {}
        self.watermark = {}

    def sbuf(self, name, shape, dtype):
        return self.stack.enter_context(self.nc.sbuf_tensor(name, shape, dtype))

    def psum(self, name, shape, dtype=F32):
        return self.stack.enter_context(self.nc.psum_tensor(name, shape, dtype))

    def _sem(self, key):
        kind, e = key
        d = self.csem if kind == "c" else self.dsem
        if e not in d:
            d[e] = self.stack.enter_context(self.nc.semaphore(f"{kind}_{e}"))
        return d[e]

    def _collect(self, engine, reads, writes):
        stamps = set()
        for b in reads:
            if b.writer is not None:
                stamps.add(b.writer)
        for b in writes:
            if b.writer is not None:
                stamps.add(b.writer)
            for r in b.readers:
                stamps.add(r)
        waits = []
        for kind, src, seq in stamps:
            key = (kind, src)
            val = seq + 1 if kind == "c" else 16 * self.dcount[src]
            wm = self.watermark.get((engine, key), 0)
            if wm >= val:
                continue
            self.watermark[(engine, key)] = val
            waits.append((key, val))
        return waits

    def _record(self, stamp, reads, writes):
        for b in reads:
            b.readers.append(stamp)
        for b in writes:
            b.writer = stamp
            b.readers = []

    def op(self, engine, emit, reads=(), writes=(), tag=""):
        waits = self._collect(engine, reads, writes)
        seq = self.ccount[engine]
        self.ccount[engine] += 1
        self.items[engine].append((waits, emit, ("c", engine), 1, tag))
        self._record(("c", engine, seq), reads, writes)

    def dma(self, queue, out, in_, reads=(), writes=(), tag="", sem_group=None,
            **kw):
        assert writes or sem_group, tag
        grp = f"{queue}_{sem_group or writes[0].name}"
        waits = self._collect(queue, reads, writes)
        seq = self.dcount.get(grp, 0)
        self.dcount[grp] = seq + 1

        def emit(eng, out=out, in_=in_, kw=kw):
            return eng.dma_start(out=out, in_=in_, **kw)

        self.items[queue].append((waits, emit, ("d", grp), 16, tag))
        self._record(("d", grp, seq), reads, writes)

    def build(self):
        nc = self.nc
        with nc.Block() as block:
            for ename in ENGINES:
                if not self.items[ename]:
                    continue

                def make_body(ename=ename):
                    def body(eng):
                        for waits, emit, inc_key, inc_n, tag in self.items[ename]:
                            for key, val in waits:
                                eng.wait_ge(self._sem(key), val)
                            emit(eng).then_inc(self._sem(inc_key), inc_n)
                        if ename == "sync":
                            for grp, cnt in self.dcount.items():
                                eng.wait_ge(self._sem(("d", grp)), 16 * cnt)

                    return body

                getattr(block, ename)(make_body())
        return nc


def build_program(n_seq=2, S=2048, n_cores=8, stage=99):
    NT = n_seq * S
    NC_ = S // CH
    TT1 = max(1, NT // 512)
    nc = bass.Bass("TRN2", target_bir_lowering=False)
    p = Prog(nc)

    def din(name, shape, dt):
        return nc.dram_tensor(name, shape, dt, kind="ExternalInput").ap()

    x_in = din("x", [NT, DM], F32)
    Wq = din("Wq", [DM, DM], F32)
    Wk = din("Wk", [DM, DM], F32)
    Wv = din("Wv", [DM, DM], F32)
    Wo = din("Wo", [DM, DM], F32)
    Wg = din("Wg", [DM, DM], F32)
    Wog = din("Wog", [DM, DM], F32)
    Wd = din("Wd", [H, DM], F32)
    bd = din("bd", [H, 1], F32)
    nw = din("nw", [1, DM], F32)
    w_gate = din("w_gate", [NI, DM], F32)
    w_up = din("w_up", [NI, DM], F32)
    w_down = din("w_down", [DM, NI], F32)
    idbf_d = din("idbf", [128, 128], BF16)
    idf_d = din("idf32", [128, 128], F32)
    idt_d = din("idt", [128, 1024], BF16)
    sel_d = din("sel", [2, 16, 1024], BF16)
    ones16_d = din("ones16", [16, 128], F32)
    aconst_d = din("aconst", [1, 8], F32)
    pm_d = din("pmask", [16, 8], F32)
    eo_d = din("eomask", [16, 128], F32)

    out_d = nc.dram_tensor("out", [NT, DM], F32, kind="ExternalOutput").ap()

    x1_s = nc.dram_tensor("x1_s", [NT, DM], F32).ap()
    ln2t_s = nc.dram_tensor("ln2t_s", [8, 128, NT], BF16).ap()
    upt_s = nc.dram_tensor("upt_s", [NIC, 128, NT], BF16).ap()
    gat_s = nc.dram_tensor("gat_s", [NIC, 128, NT], BF16).ap()
    tra_s = nc.dram_tensor("tra_s", [NIC, 128, NT], BF16).ap()
    ora_s = nc.dram_tensor("ora_s", [NIC, 128, NT], BF16).ap()
    wut_s = nc.dram_tensor("wut_s", [NIC, 128, 1024], BF16).ap()
    wgt_s = nc.dram_tensor("wgt_s", [NIC, 128, 1024], BF16).ap()
    wdnt_s = nc.dram_tensor("wdnt_s", [NIC, 128, 1024], BF16).ap()
    cc_in = [nc.dram_tensor(f"cc{i}_in", [1, 2], F32) for i in range(4)]
    cc_out = [nc.dram_tensor(f"cc{i}_out", [1, 2], F32) for i in range(4)]

    AB_COLS = 72064
    AB = p.sbuf("AB", [128, AB_COLS], BF16)
    ab_off = [0]

    def abreg(cols, name):
        o = ab_off[0]
        ab_off[0] += cols
        assert ab_off[0] <= AB_COLS, (name, ab_off[0])
        return o

    def abv(off, cols, dt=BF16):
        ap = AB[:, off:off + cols]
        if dt is not BF16:
            ap = ap.bitcast(dt)
        return ap

    o_wt = {w: abreg(8192, w) for w in ("q", "k", "v", "g", "og")}
    o_wot = abreg(8192, "wot")
    o_wdt = abreg(128, "wdt")
    o_stage = abreg(8192, "stage")
    o_ln = abreg(1024, "ln")
    o_lnT = abreg(1024, "lnT")
    o_q = abreg(1024, "q")
    o_thxg = abreg(1024, "thxg")
    o_kt = abreg(1024, "kt")
    o_v = abreg(1024, "v")
    o_thog = abreg(1024, "thog")
    o_kT = abreg(1024, "kT")
    o_qT = abreg(1024, "qT")
    o_Dt = abreg(1024, "Dt")
    o_At = abreg(1024, "At")
    o_k2 = abreg(1024, "k2")
    o_lnT2 = abreg(1024, "lnT2")
    o_s0bf = abreg(1024, "s0bf")
    o_rhl = abreg(256, "rhl")
    OB1S = o_wt["og"]
    OB2S = 0
    OB_WDN = o_wt["og"] + 8192
    OB_XO = 2 * 4096

    AF_COLS = 12288
    AFA = p.sbuf("AFA", [128, AF_COLS], F32)
    af_off = [0]

    def afreg(cols, name):
        o = af_off[0]
        af_off[0] += cols
        assert af_off[0] <= AF_COLS, (name, af_off[0])
        return o

    def afv(off, cols):
        return AFA[:, off:off + cols]

    o_x = [afreg(1024, f"x{i}") for i in range(2)]
    o_attn = afreg(1024, "attn")
    o_tmpO = afreg(1024, "tmpO")
    o_x1 = afreg(1024, "x1")
    o_S0 = afreg(1024, "S0")
    o_sm = afreg(96, "sm")
    o_g = afreg(512, "gtiles")
    o_am = afreg(80, "am")
    o_nw = afreg(16, "nw")
    o_sc = afreg(64, "scales")
    o_wsc = afreg(192, "wsc")
    o_wraw = afreg(2048, "wraw")
    o_dq = afreg(3072, "dq")

    t_bd = p.sbuf("t_bd", [16, 1], F32)
    t_c16 = p.sbuf("t_c16", [16, 160], F32)
    t_id = p.sbuf("t_id", [128, 128], BF16)
    t_idf = p.sbuf("t_idf", [128, 128], F32)
    t_idt = p.sbuf("t_idt", [128, 1024], BF16)
    t_sel = p.sbuf("t_sel", [16, 2048], BF16)
    t_ones16 = p.sbuf("t_ones", [16, 128], F32)
    t_pm = p.sbuf("t_pm", [16, 8], F32)
    t_eo = p.sbuf("t_eo", [16, 128], F32)
    t_wd16 = p.sbuf("t_wd16", [16, 1024], BF16)
    t_nwrow = p.sbuf("t_nwrow", [1, 1024], F32)
    t_g64 = p.sbuf("t_g64", [64, 128], F32)
    t_ac = p.sbuf("t_ac", [128, 8], F32)

    PP = [p.psum(f"PP{i}", [128, 1024], F32) for i in range(4)]

    B = {}

    def gb(name):
        if name not in B:
            B[name] = Buf(name)
        return B[name]

    pb = [gb(f"pb{i}") for i in range(8)]

    def pp(i, half=None):
        if half is None:
            return PP[i][:, :], [pb[2 * i], pb[2 * i + 1]]
        return PP[i][:, 512 * half:512 * (half + 1)], [pb[2 * i + half]]

    ACONSTS = [LN_EPS, RMS_EPS, 1.0, -1.0, 0.5, -0.5, 0.0, 1.0 / 65536.0]
    p.dma("gpsimd", out=t_ac[:, :],
          in_=aconst_d.partition_broadcast(128).squeeze(1), writes=[gb("actc")],
          sem_group="init")
    for ci, cv in enumerate(ACONSTS):
        nc.const_aps.register(F32, cv, t_ac[:, ci:ci + 1])
    for _e in ("tensor", "vector", "scalar", "gpsimd"):
        p.op(_e, lambda eng: eng.nop(), reads=[gb("actc")], tag=f"acwait_{_e}")

    for ap_, t_, bn in ((idbf_d, t_id, "id"), (idf_d, t_idf, "idf"),
                        (idt_d, t_idt, "idt"), (ones16_d, t_ones16, "ones16"),
                        (pm_d, t_pm, "pm"), (eo_d, t_eo, "eo"),
                        (bd, t_bd, "bd"), (nw, t_nwrow, "nwrow")):
        p.dma("sync", out=t_[tuple(slice(None) for _ in t_.shape)], in_=ap_,
              writes=[gb(bn)], sem_group="init")
    p.dma("sync", out=t_sel[:, :].rearrange("a (g c) -> a g c", g=2),
          in_=sel_d.rearrange("g k c -> k g c"), writes=[gb("sel")],
          sem_group="init")

    nwcol = afv(o_nw, 8)

    def emit_nwT(t):
        last = None
        for k in range(8):
            last = t.transpose(PP[3][:, 64 + k:64 + k + 1],
                               t_nwrow[:, 128 * k:128 * (k + 1)], t_idf[0:1, 0:1])
        return last
    p.op("tensor", emit_nwT, reads=[gb("nwrow"), gb("idf")], writes=[pb[6]])
    p.op("vector", lambda v: v.tensor_scalar(
        out=nwcol[:, :], in0=PP[3][:, 64:72], scalar1=0.5, scalar2=None,
        op0=ALU.mult), reads=[pb[6]], writes=[gb("nw")])

    stage_sl = [abv(o_stage, 8192),
                AFA[:, o_wraw:o_wraw + 4096].bitcast(BF16)]
    wq_rr = ["gpsimd", "gpsimd", "gpsimd"]
    wp_ct = [0]

    def prep_weight(wname, src, out_off, scale_col=None):
        wi = wp_ct[0]
        wp_ct[0] += 1
        stage = stage_sl[wi % 2]
        bs = gb("stage0") if wi % 2 == 0 else gb("stageB")
        p.dma(wq_rr[wi % 3],
              out=stage.rearrange("p (i d) -> p i d", i=8),
              in_=src.rearrange("(i p) d -> p i d", p=128),
              writes=[bs], tag=f"w{wname}ld")
        wt_b = gb("wt_" + wname)
        for k in range(8):
            def tp(t, stage=stage, k=k):
                last = None
                for i in range(8):
                    last = t.transpose(
                        PP[3][:, 0:512].bitcast(BF16)[:, 128 * i:128 * (i + 1)],
                        stage[:, 1024 * i + 128 * k: 1024 * i + 128 * (k + 1)],
                        t_id[:, :])
                return last
            p.op("tensor", tp, reads=[bs, gb("id")], writes=[pb[6]],
                 tag=f"w{wname}tp{k}")
            dst = abv(out_off, 8192).rearrange("p (k o) -> p k o", k=8)[:, k, :]
            if scale_col is None:
                p.op("scalar", lambda s, dst=dst: s.copy(
                    out=dst, in_=PP[3][:, 0:512].bitcast(BF16)),
                    reads=[pb[6]], writes=[wt_b], tag=f"w{wname}ev{k}")
            else:
                p.op("scalar", lambda s, dst=dst, k=k: s.activation(
                    out=dst, in_=PP[3][:, 0:512].bitcast(BF16), func=AF.Identity,
                    scale=nwcol[:, k:k + 1], bias=0.0),
                    reads=[pb[6], gb("nw")], writes=[wt_b], tag=f"w{wname}ev{k}")
        return wt_b

    bw = {}
    bw["q"] = prep_weight("q", Wq, o_wt["q"])
    bw["k"] = prep_weight("k", Wk, o_wt["k"])
    bw["g"] = prep_weight("g", Wg, o_wt["g"])

    p.dma("gpsimd", out=t_wd16[:, :], in_=Wd, writes=[gb("wd16")],
          sem_group="init")

    def wd_tp(t):
        last = None
        for k in range(8):
            last = t.transpose(
                PP[3][:, 0:512].bitcast(BF16)[:, 16 * k:16 * (k + 1)],
                t_wd16[:, 128 * k:128 * (k + 1)], t_id[0:16, 0:16])
        return last
    p.op("tensor", wd_tp, reads=[gb("wd16"), gb("id")], writes=[pb[6]])
    p.op("scalar", lambda s: s.copy(out=abv(o_wdt, 128),
         in_=PP[3][:, 0:64].bitcast(BF16)), reads=[pb[6]], writes=[gb("wdt")])

    am64 = afv(o_am, 64)
    S0 = afv(o_S0, 1024)
    S0bf = abv(o_s0bf, 1024)
    p.op("vector", lambda v: v.memset(S0[:, :], 0.0), writes=[gb("S0")])
    p.op("vector", lambda v: v.memset(S0bf[:, :], 0.0), writes=[gb("S0bf")])
    p.op("vector", lambda v: v.memset(am64[:, :], 0.0), writes=[gb("am64")])

    wraw = [afv(o_wraw, 1024), afv(o_wraw + 1024, 1024)]
    bwraw = [gb("wraw0"), gb("wraw1")]
    wsc = afv(o_wsc, 96)
    wrec = afv(o_wsc + 96, 96)
    dq0_sl = [afv(o_dq, 1024), afv(o_dq + 1024, 1024)]
    dq2 = afv(o_dq + 2048, 1024)
    wqi_sl = [abv(o_stage, 1024, I16), abv(o_stage + 7168, 1024, I16)]
    wdq_st = [abv(o_stage + 1024 + 1024 * j, 1024) for j in range(2)] + \
        [abv(o_stage + 6144, 1024)]
    bwdq = [gb("wdq0"), gb("wdq1"), gb("wdq2")]
    wev_st = [abv(o_stage + 3072 + 1024 * j, 1024) for j in range(2)]
    bwev = [gb("wev0"), gb("wev1")]
    def emit_stfence():
        p.op("vector", lambda v: v.memset(AFA[0:1, o_sc:o_sc + 1], 0.0),
             reads=[],
             writes=[gb("stage0"), gb("stageB"),
                     gb("wqi0"), gb("wqi1"), bwdq[0], bwdq[1], bwdq[2],
                     bwev[0], bwev[1], bwraw[0], bwraw[1],
                     gb("dq00"), gb("dq01")], tag="stfence")

    def front_dma(uidx):
        wi = uidx % 2
        p.dma("gpsimd", out=wraw[wi][:, :], in_=units[uidx][0],
              writes=[bwraw[wi]], tag=f"wu_ld{uidx}")

    def front_calc_a(uidx):
        wi = uidx % 2
        braw = bwraw[wi]
        sl = 32 * (uidx % 3)
        vsc = wraw[wi][:, :].rearrange("p (b c) -> p b c", c=32)
        wsc_u = wsc[:, sl:sl + 32]
        wrec_u = wrec[:, sl:sl + 32]
        recb = wrec_u.unsqueeze(2).broadcast_to([128, 32, 32])
        wqi_u = wqi_sl[uidx % 2]
        bwqi_u = gb(f"wqi{uidx % 2}")
        bws = gb(f"wsc{uidx % 3}")
        bwr = gb(f"wrec{uidx % 3}")
        p.op("vector", lambda v: v.tensor_reduce(out=wsc_u, in_=vsc,
             axis=AX.X, op=ALU.max, apply_absolute_value=True),
             reads=[braw], writes=[bws], tag=f"wu_mx{uidx}")
        p.op("vector", lambda v: v.tensor_scalar(out=wsc_u,
             in0=wsc_u, scalar1=1.0 / 7.0, scalar2=1e-12, op0=ALU.mult,
             op1=ALU.max), reads=[bws], writes=[bws])
        p.op("vector", lambda v: v.reciprocal(out=wrec_u,
             in_=wsc_u), reads=[bws], writes=[bwr])
        p.op("vector", lambda v, wqi_u=wqi_u: v.tensor_tensor(
             out=wqi_u.rearrange("p (b c) -> p b c", c=32), in0=vsc, in1=recb,
             op=ALU.mult), reads=[braw, bwr], writes=[bwqi_u])
        dq0_u = dq0_sl[uidx % 2]
        bdq0_u = gb(f"dq0{uidx % 2}")
        p.op("gpsimd", lambda g, dq0_u=dq0_u: g.tensor_tensor(
             out=dq0_u[:, :].rearrange("p (b c) -> p b c", c=32), in0=vsc,
             in1=recb, op=ALU.mult), reads=[braw, bwr],
             writes=[bdq0_u])
        p.op("gpsimd", lambda g, wqi_u=wqi_u, dq0_u=dq0_u: g.tensor_tensor(
             out=dq0_u[:, :], in0=dq0_u[:, :], in1=wqi_u, op=ALU.subtract),
             reads=[bdq0_u, bwqi_u], writes=[bdq0_u])

    def front_calc_b(uidx):
        sl = 32 * (uidx % 3)
        scb = wsc[:, sl:sl + 32].unsqueeze(2).broadcast_to([128, 32, 32])
        wdq_u = wdq_st[uidx % 3]
        bwdq_u = bwdq[uidx % 3]
        wqi_u = wqi_sl[uidx % 2]
        bwqi_u = gb(f"wqi{uidx % 2}")
        dq0_u = dq0_sl[uidx % 2]
        bdq0_u = gb(f"dq0{uidx % 2}")
        p.op("vector", lambda v, dq0_u=dq0_u: v.tensor_scalar(out=dq2[:, :],
             in0=dq0_u[:, :], scalar1=0.5, scalar2=None, op0=ALU.is_ge),
             reads=[bdq0_u], writes=[gb("dq2")])
        p.op("vector", lambda v, dq0_u=dq0_u: v.scalar_tensor_tensor(
             out=dq2[:, :], in0=dq0_u[:, :], scalar=-0.5, in1=dq2[:, :],
             op0=ALU.is_le, op1=ALU.subtract),
             reads=[bdq0_u, gb("dq2")], writes=[gb("dq2")])
        p.op("gpsimd", lambda g, wqi_u=wqi_u: g.tensor_tensor(out=dq2[:, :],
             in0=wqi_u, in1=dq2[:, :], op=ALU.subtract),
             reads=[gb("dq2"), bwqi_u], writes=[gb("dq2")])
        p.op("gpsimd", lambda g, wdq_u=wdq_u: g.tensor_tensor(
             out=wdq_u.rearrange("p (b c) -> p b c", c=32),
             in0=dq2[:, :].rearrange("p (b c) -> p b c", c=32), in1=scb,
             op=ALU.mult), reads=[gb("dq2"), gb(f"wsc{uidx % 3}")],
             writes=[bwdq_u])

    def unit_back(uidx, dst_ap, bdst):
        wdq_u = wdq_st[uidx % 3]
        bwdq_u = bwdq[uidx % 3]
        hh = uidx % 2
        ph = PP[1][:, 512 * hh:512 * (hh + 1)]
        pbh = pb[2 + hh]

        def tp(t, wdq_u=wdq_u, ph=ph):
            last = None
            for k in range(8):
                last = t.transpose(
                    ph.bitcast(BF16)[:, 128 * k:128 * (k + 1)],
                    wdq_u[:, 128 * k:128 * (k + 1)], t_id[:, :])
            return last
        p.op("tensor", tp, reads=[bwdq_u, gb("id")], writes=[pbh],
             tag=f"wu_tp{uidx}")
        ev = wev_st[uidx % 2]
        p.op("scalar", lambda s, ev=ev, ph=ph: s.copy(out=ev[:, :],
             in_=ph.bitcast(BF16)), reads=[pbh],
             writes=[bwev[uidx % 2]], tag=f"wu_ev{uidx}")
        p.dma("scalar", out=dst_ap, in_=ev[:, :], reads=[bwev[uidx % 2]],
              writes=[bdst], tag=f"wu_st{uidx}")

    units = []
    for i in range(NIC):
        units.append((w_up[128 * i:128 * (i + 1), :], wut_s[i], gb("wuts")))
        units.append((w_gate[128 * i:128 * (i + 1), :], wgt_s[i], gb("wgts")))
    for r in range(8):
        for ii in range(3):
            units.append((w_down[128 * r:128 * (r + 1),
                                 1024 * ii:1024 * (ii + 1)], None, (r, ii)))
    dma_ct = [0]
    aca_ct = [0]
    acb_ct = [0]
    back_ct = [0]

    NU = len(units)
    NTOT = n_seq * NC_

    def unit_dst(u):
        src, dst, bdst = units[u]
        if dst is not None:
            return dst, bdst
        r, ii = bdst
        dst_ap = wdnt_s.rearrange("i p d -> p i d")[
            :, 8 * ii:8 * (ii + 1), 128 * r:128 * (r + 1)]
        return dst_ap, gb("wdns")

    def emit_fdma():
        u = dma_ct[0]
        if u >= len(units) or u > aca_ct[0] + 1:
            return
        dma_ct[0] += 1
        front_dma(u)

    def emit_fa():
        u = aca_ct[0]
        if u >= dma_ct[0] or u > acb_ct[0] + 1:
            return
        aca_ct[0] += 1
        front_calc_a(u)

    def emit_fb():
        u = acb_ct[0]
        if u >= aca_ct[0] or u > back_ct[0] + 2:
            return
        acb_ct[0] += 1
        front_calc_b(u)

    def emit_back():
        u = back_ct[0]
        if u >= acb_ct[0] or u >= len(units):
            return
        back_ct[0] += 1
        dst_ap, bdst = unit_dst(u)
        unit_back(u, dst_ap, bdst)

    WT = {w: abv(o_wt[w], 8192).rearrange("p (k o) -> p k o", k=8)
          for w in ("q", "k", "v", "g", "og")}
    WOT = abv(o_wot, 8192).rearrange("p (k o) -> p k o", k=8)
    WDT = abv(o_wdt, 128).rearrange("p (k h) -> p k h", k=8)

    x_t = [afv(o, 1024) for o in o_x]
    bx = [gb(f"bx{i}") for i in range(2)]
    ln = abv(o_ln, 1024)
    lnT_sl = [abv(o_lnT, 1024), abv(o_stage + 5120, 1024)]
    q_tok = abv(o_q, 1024)
    thxg = abv(o_thxg, 1024)
    kt = abv(o_kt, 1024)
    v_h = abv(o_v, 1024)
    thog = abv(o_thog, 1024)
    kT = abv(o_kT, 1024)
    qT = abv(o_qT, 1024)
    Dt = abv(o_Dt, 1024)
    At = abv(o_At, 1024)
    k2 = abv(o_k2, 1024)
    z_bf = abv(o_q, 1024)
    y_bf = abv(o_thxg, 1024)
    yT = abv(o_kt, 1024)
    ln2 = abv(o_v, 1024)
    lnT2 = abv(o_lnT2, 1024)
    attn = afv(o_attn, 1024)
    tmpO = afv(o_tmpO, 1024)
    x1_t = afv(o_x1, 1024)
    rhi = abv(o_rhl, 128)
    rlo = abv(o_rhl + 128, 128)

    SMS = [afv(o_sm + 32 * j, 32) for j in range(3)]

    c16 = t_c16
    sp16 = AFA[0:16, o_g:o_g + 128]
    expg0 = AFA[0:16, o_g + 128:o_g + 256]
    GT = t_g64[0:16, :]
    GmT = t_g64[32:48, :]
    expG = AFA[:, o_g + 256:o_g + 272]
    E2 = AFA[:, o_g + 272:o_g + 288]
    expGCp = AFA[:, o_g + 288:o_g + 296]

    def ln1_stats(s, c, idx):
        xi = x_t[idx % 2]
        bxi = bx[idx % 2]
        sm = SMS[0]
        bns = sm[:, 6:18]
        p.op("vector", lambda v: v.bn_stats(out=bns[:, 0:6], in_=xi[:, 0:512]),
             reads=[bxi], writes=[gb("bns0")])
        p.op("vector", lambda v: v.bn_stats(out=bns[:, 6:12], in_=xi[:, 512:1024]),
             reads=[bxi], writes=[gb("bns0")])
        p.op("vector", lambda v: v.bn_aggr(out=sm[:, 4:6], in_=bns[:, :]),
             reads=[gb("bns0")], writes=[gb("mv0")])
        p.op("scalar", lambda s_: s_.activation(out=sm[:, 2:3], in_=sm[:, 5:6],
             func=AF.Ln, bias=LN_EPS, scale=1.0), reads=[gb("mv0")],
             writes=[gb("lv0")])
        p.op("scalar", lambda s_: s_.activation(out=sm[:, 0:1], in_=sm[:, 2:3],
             func=AF.Exp, bias=0.0, scale=-0.5), reads=[gb("lv0")],
             writes=[gb("rstd0")])

    def ln1_apply(s, c, idx):
        xi = x_t[idx % 2]
        bxi = bx[idx % 2]
        sm = SMS[0]
        p.op("vector", lambda v: v.tensor_scalar(out=sm[:, 1:2], in0=sm[:, 4:5],
             scalar1=sm[:, 0:1], scalar2=-1.0, op0=ALU.mult, op1=ALU.mult),
             reads=[gb("mv0"), gb("rstd0")], writes=[gb("negmr0")])
        p.op("vector", lambda v: v.tensor_scalar(out=ln[:, :], in0=xi[:, :],
             scalar1=sm[:, 0:1], scalar2=sm[:, 1:2], op0=ALU.mult,
             op1=ALU.add),
             reads=[bxi, gb("rstd0"), gb("negmr0"), gb("ln")], writes=[gb("ln")])

    ldps = PP[3][0:16, 512:640]

    def emit_lnT(idx):
        lnT = lnT_sl[idx % 2]
        blnT = gb(f"lnT{idx % 2}")

        def lnT_tp(t):
            last = None
            for k in range(8):
                last = t.transpose(
                    PP[3][:, 0:512].bitcast(BF16)[:, 128 * k:128 * (k + 1)],
                    ln[:, 128 * k:128 * (k + 1)], t_id[:, :])
            return last
        p.op("tensor", lnT_tp, reads=[gb("ln"), gb("id")], writes=[pb[6]],
             tag=f"lnTtp{idx}")
        p.op("scalar", lambda s_, lnT=lnT: s_.copy(out=lnT[:, :],
             in_=PP[3][:, 0:512].bitcast(BF16)), reads=[pb[6]], writes=[blnT],
             tag=f"lnTev{idx}")

    def emit_ld(idx):
        lnTv = lnT_sl[idx % 2].rearrange("p (k t) -> p k t", k=8)
        blnT = gb(f"lnT{idx % 2}")

        def ldmm(t, lnTv=lnTv):
            last = None
            for k in range(8):
                last = t.matmul(ldps, WDT[:, k, :], lnTv[:, k, :],
                                start=(k == 0), stop=(k == 7))
            return last
        p.op("tensor", ldmm, reads=[blnT, gb("wdt")], writes=[pb[7]],
             tag=f"ldmm{idx}")

    def emit_qkg(idx):
        lnTv = lnT_sl[idx % 2].rearrange("p (k t) -> p k t", k=8)
        blnT = gb(f"lnT{idx % 2}")

        def qg(t, lnTv=lnTv):
            last = None
            for k in range(8):
                st, sp_ = (k == 0), (k == 7)
                for (wn, ppi) in (("q", 0), ("g", 2)):
                    t.matmul(PP[ppi][:, 0:512], lnTv[:, k, :],
                             WT[wn][:, k, 0:512], start=st, stop=sp_)
                    last = t.matmul(PP[ppi][:, 512:1024], lnTv[:, k, :],
                                    WT[wn][:, k, 512:1024], start=st, stop=sp_)
            return last
        p.op("tensor", qg, reads=[blnT, bw["q"], bw["g"]],
             writes=[pb[0], pb[1], pb[4], pb[5]], tag=f"qg{idx}")

        def kk(t, lnTv=lnTv):
            last = None
            for k in range(8):
                st, sp_ = (k == 0), (k == 7)
                t.matmul(PP[1][:, 0:512], lnTv[:, k, :],
                         WT["k"][:, k, 0:512], start=st, stop=sp_)
                last = t.matmul(PP[1][:, 512:1024], lnTv[:, k, :],
                                WT["k"][:, k, 512:1024], start=st, stop=sp_)
            return last
        p.op("tensor", kk, reads=[blnT, bw["k"]],
             writes=[pb[2], pb[3]], tag=f"kk{idx}")

    a16 = c16[:, 0:128]
    expGC16 = c16[:, 144:145]
    prhs = c16[:, 152:160]

    def ldg_front(idx):
        p.op("scalar", lambda s_: s_.activation(out=a16[:, :], in_=ldps,
             func=AF.Abs, bias=t_bd[:, :], scale=1.0),
             reads=[pb[7], gb("bd")], writes=[gb("a16")])
        p.op("scalar", lambda s_: s_.activation(out=sp16[:, :], in_=ldps,
             func=AF.Relu, bias=t_bd[:, :], scale=1.0),
             reads=[pb[7], gb("bd")], writes=[gb("sp")])
        p.op("scalar", lambda s_: s_.activation(out=a16[:, :], in_=a16[:, :],
             func=AF.Exp, bias=0.0, scale=-1.0), reads=[gb("a16")],
             writes=[gb("a16")])
        p.op("scalar", lambda s_: s_.activation(out=a16[:, :], in_=a16[:, :],
             func=AF.Ln, bias=1.0, scale=1.0), reads=[gb("a16")],
             writes=[gb("a16")])
        p.op("vector", lambda v: v.tensor_tensor(out=sp16[:, :], in0=sp16[:, :],
             in1=a16[:, :], op=ALU.add), reads=[gb("a16"), gb("sp")],
             writes=[gb("sp")])
        p.op("vector", lambda v: v.tensor_tensor_scan(out=GT[:, :],
             data0=t_ones16[:, :], data1=sp16[:, :], initial=0.0,
             op0=ALU.mult, op1=ALU.subtract),
             reads=[gb("sp"), gb("ones16")], writes=[gb("GT")],
             tag=f"gscan{idx}")
        p.op("scalar", lambda s_: s_.activation(out=expg0[:, :], in_=sp16[:, :],
             func=AF.Exp, bias=0.0, scale=-1.0), reads=[gb("sp")],
             writes=[gb("expg")])
        p.op("vector", lambda v: v.memset(expg0[:, 0:1], 0.0),
             writes=[gb("expg")])
        p.op("scalar", lambda s_: s_.copy(out=rhi[0:16, :], in_=expg0[:, :]),
             reads=[gb("expg")], writes=[gb("rhi")])
        p.op("vector", lambda v: v.tensor_tensor(out=rlo[0:16, :],
             in0=expg0[:, :], in1=rhi[0:16, :], op=ALU.subtract),
             reads=[gb("expg"), gb("rhi")], writes=[gb("rlo")])
        p.op("vector", lambda v: v.tensor_scalar(out=GmT[:, :], in0=GT[:, :],
             scalar1=GT[:, 127:128], scalar2=None, op0=ALU.subtract),
             reads=[gb("GT")], writes=[gb("GmT")])
        p.op("scalar", lambda s_: s_.activation(out=expGC16[:, :],
             in_=GT[:, 127:128], func=AF.Exp, bias=0.0, scale=1.0),
             reads=[gb("GT")], writes=[gb("gc16")])
        p.op("vector", lambda v: v.tensor_scalar(out=prhs[:, :], in0=t_pm[:, :],
             scalar1=expGC16[:, :], scalar2=None, op0=ALU.mult),
             reads=[gb("gc16"), gb("pm")], writes=[gb("prhs")])

    def ldg_back(idx):
        p.op("tensor", lambda t: t.transpose(PP[3][:, 512:576], t_g64[:, :],
             t_idf[0:64, 0:64]), reads=[gb("GT"), gb("GmT"), gb("idf")],
             writes=[pb[7]], tag=f"ge2tp{idx}")
        p.op("scalar", lambda s_: s_.activation(out=expG[:, :],
             in_=PP[3][:, 512:528], func=AF.Exp, bias=0.0, scale=1.0),
             reads=[pb[7]], writes=[gb("expG")])
        p.op("scalar", lambda s_: s_.activation(out=E2[:, :],
             in_=PP[3][:, 544:560], func=AF.Exp, bias=0.0, scale=-1.0),
             reads=[pb[7]], writes=[gb("E2")])
        p.op("tensor", lambda t: t.matmul(PP[3][:, 576:584], t_eo[:, :],
             prhs[:, :], start=True, stop=True), reads=[gb("prhs"), gb("eo")],
             writes=[pb[7]], tag=f"gcmm{idx}")

    def expgcp_emit(idx):
        p.op("vector", lambda v: v.tensor_copy(out=expGCp[:, :],
             in_=PP[3][:, 576:584]), reads=[], writes=[pb[7], gb("expGCp")],
             tag=f"egcp{idx}")

    def rmm_g(g, idx):
        ppR = PP[2 + g]
        pbR = [pb[4 + 2 * g], pb[5 + 2 * g]]

        def rmm(t, g=g, ppR=ppR):
            last = None
            for j in range(8):
                sel = t_sel[:, 1024 * g + 128 * j: 1024 * g + 128 * (j + 1)]
                t.matmul(ppR[:, 128 * j:128 * (j + 1)], sel, rhi[0:16, :],
                         start=True, stop=False)
                last = t.matmul(ppR[:, 128 * j:128 * (j + 1)], sel,
                                rlo[0:16, :], start=False, stop=True)
            return last
        p.op("tensor", rmm, reads=[gb("rhi"), gb("rlo"), gb("sel")],
             writes=pbR, tag=f"rmm{g}_{idx}")

    def scan_g(g, idx):
        ppR = PP[2 + g]
        pbR = [pb[4 + 2 * g], pb[5 + 2 * g]]
        dt_g = Dt if g == 0 else At
        bdt = gb("Dt") if g == 0 else gb("At")
        p.op("vector", lambda v, ppR=ppR, dt_g=dt_g: v.tensor_tensor_scan(
             out=dt_g[:, :], data0=ppR[:, :], data1=t_idt[:, :], initial=0.0,
             op0=ALU.mult, op1=ALU.add),
             reads=pbR + [gb("idt")], writes=[bdt], tag=f"scan{g}_{idx}")

    def fslot(want, n=1):
        for _ in range(n):
            if aca_ct[0] < want:
                emit_fb()
                emit_fa()
                emit_fdma()

    def chunk(s, c, idx, handoff=None, pre=None, post=None):
        xi = x_t[idx % 2]
        bxi = bx[idx % 2]
        row0 = s * S + c * CH
        sfx = f"_{s}_{c}"

        emit_fdma()
        want_u = (idx + 1) * NU // NTOT + 1

        p.op("scalar", lambda s_: s_.activation(out=thxg[:, :], in_=PP[2][:, :],
             func=AF.Tanh, bias=0.0, scale=0.5),
             reads=[pb[4], pb[5], gb("thxg")], writes=[gb("thxg")])
        p.op("scalar", lambda s_: s_.copy(out=q_tok[:, :], in_=PP[0][:, :]),
             reads=[pb[0], pb[1], gb("q")], writes=[gb("q")])
        p.op("vector", lambda v: v.scalar_tensor_tensor(out=kt[:, :],
             in0=thxg[:, :], scalar=1.0, in1=PP[1][:, :], op0=ALU.add,
             op1=ALU.mult), reads=[gb("thxg"), pb[2], pb[3], gb("kt")],
             writes=[gb("kt")])

        def kq_tp(t):
            last = None
            for j in range(8):
                t.transpose(PP[0][:, 0:512].bitcast(BF16)[:, 128 * j:128 * (j + 1)],
                            kt[:, 128 * j:128 * (j + 1)], t_id[:, :])
                last = t.transpose(
                    PP[0][:, 512:1024].bitcast(BF16)[:, 128 * j:128 * (j + 1)],
                    q_tok[:, 128 * j:128 * (j + 1)], t_id[:, :])
            return last
        p.op("tensor", kq_tp, reads=[gb("kt"), gb("q"), gb("id")],
             writes=[pb[0], pb[1]])
        p.op("scalar", lambda s_: s_.copy(out=kT[:, :],
             in_=PP[0][:, 0:512].bitcast(BF16)), reads=[pb[0]], writes=[gb("kT")])
        p.op("scalar", lambda s_: s_.copy(out=qT[:, :],
             in_=PP[0][:, 512:1024].bitcast(BF16)), reads=[pb[1]],
             writes=[gb("qT")])
        rmm_g(0, idx)
        scan_g(0, idx)
        fslot(want_u, n=2)

        lnTv = lnT_sl[idx % 2].rearrange("p (k t) -> p k t", k=8)
        blnT = gb(f"lnT{idx % 2}")

        def pass2(t, lnTv=lnTv):
            last = None
            for k in range(8):
                st, sp_ = (k == 0), (k == 7)
                t.matmul(PP[0][:, 0:512], lnTv[:, k, :], WT["v"][:, k, 0:512],
                         start=st, stop=sp_)
                t.matmul(PP[0][:, 512:1024], lnTv[:, k, :],
                         WT["v"][:, k, 512:1024], start=st, stop=sp_)
                t.matmul(PP[1][:, 0:512], lnTv[:, k, :], WT["og"][:, k, 0:512],
                         start=st, stop=sp_)
                last = t.matmul(PP[1][:, 512:1024], lnTv[:, k, :],
                                WT["og"][:, k, 512:1024], start=st, stop=sp_)
            return last
        p.op("tensor", pass2, reads=[blnT, bw["v"], bw["og"]],
             writes=[pb[0], pb[1], pb[2], pb[3]], tag="pass2" + sfx)
        p.op("scalar", lambda s_: s_.activation(out=v_h[:, :], in_=PP[0][:, :],
             func=AF.Identity, bias=0.0, scale=0.5),
             reads=[pb[0], pb[1], gb("v")], writes=[gb("v")])
        p.op("scalar", lambda s_: s_.activation(out=thog[:, :], in_=PP[1][:, :],
             func=AF.Tanh, bias=0.0, scale=0.5), reads=[pb[2], pb[3]],
             writes=[gb("thog")])
        if pre is not None:
            pre(1)

        p.op("gpsimd", lambda g: g.tensor_tensor(
             out=k2[:, :].rearrange("p (h d) -> p h d", h=16),
             in0=kt[:, :].rearrange("p (h d) -> p h d", h=16),
             in1=E2[:, :].unsqueeze(2).broadcast_to([128, 16, 64]),
             op=ALU.mult), reads=[gb("kt"), gb("E2")], writes=[gb("k2")],
             tag="k2" + sfx)


        for g in range(2):
            ppR = PP[2 + g]
            pbR = [pb[4 + 2 * g], pb[5 + 2 * g]]

            def smm(t, g=g, ppR=ppR):
                last = None
                for j in range(8):
                    b0 = 64 * g
                    last = t.matmul(ppR[:, 128 * j:128 * (j + 1)],
                                    kT[b0:b0 + 64, 128 * j:128 * (j + 1)],
                                    qT[b0:b0 + 64, 128 * j:128 * (j + 1)],
                                    start=True, stop=True, tile_position=(b0, 0))
                return last
            p.op("tensor", smm, reads=[gb("kT"), gb("qT")], writes=pbR,
                 tag=f"smm{g}" + sfx)

        for g in range(2):
            ppR = PP[2 + g]
            pbR = [pb[4 + 2 * g], pb[5 + 2 * g]]
            dt_g = Dt if g == 0 else At
            bdt = gb("Dt") if g == 0 else gb("At")
            p.op("vector", lambda v, ppR=ppR, dt_g=dt_g: v.tensor_tensor(
                 out=dt_g[:, :], in0=ppR[:, :], in1=dt_g[:, :], op=ALU.mult),
                 reads=pbR, writes=[bdt], tag=f"att{g}" + sfx)
        if pre is not None:
            pre(2)

        def ointer(t):
            last = None
            for jj in range(8):
                for b in range(2):
                    pos = 8 * b + jj
                    last = t.matmul(PP[1][:, 64 * pos:64 * (pos + 1)],
                                    qT[64 * b:64 * b + 64,
                                       128 * jj:128 * (jj + 1)],
                                    S0bf[64 * b:64 * b + 64,
                                         512 * s + 64 * jj:512 * s + 64 * (jj + 1)],
                                    start=True, stop=True,
                                    tile_position=(64 * b, 0))
            return last
        p.op("tensor", ointer, reads=[gb("qT"), gb("S0bf")],
             writes=[pb[2], pb[3]], tag="ointer" + sfx)

        def oint(t):
            last = None
            for h in range(H):
                g = h % 2
                j = h // 2
                src = Dt if g == 0 else At
                last = t.matmul(PP[0][:, 64 * h:64 * (h + 1)],
                                src[:, 128 * j:128 * (j + 1)],
                                v_h[:, 64 * h:64 * (h + 1)],
                                start=True, stop=True)
            return last
        p.op("tensor", oint, reads=[gb("Dt"), gb("At"), gb("v")],
             writes=[pb[0], pb[1]], tag="oint" + sfx)

        def updmm(t):
            last = None
            for jj in range(8):
                for b in range(2):
                    h = 2 * jj + b
                    last = t.matmul(PP[2][64 * b:64 * b + 64,
                                          64 * jj:64 * (jj + 1)],
                                    k2[:, 64 * h:64 * (h + 1)],
                                    v_h[:, 64 * h:64 * (h + 1)],
                                    start=True, stop=True,
                                    tile_position=(0, 64 * b))
            return last
        p.op("tensor", updmm, reads=[gb("k2"), gb("v")], writes=[pb[4]],
             tag="upd" + sfx)

        def s0upd(v):
            last = None
            for jj in range(8):
                last = v.scalar_tensor_tensor(
                    out=S0[:, 512 * s + 64 * jj:512 * s + 64 * (jj + 1)],
                    in0=S0[:, 512 * s + 64 * jj:512 * s + 64 * (jj + 1)],
                    scalar=expGCp[:, jj:jj + 1],
                    in1=PP[2][:, 64 * jj:64 * (jj + 1)],
                    op0=ALU.mult, op1=ALU.add)
            return last
        p.op("vector", s0upd, reads=[pb[4], gb("expGCp"), gb("S0")],
             writes=[gb("S0")], tag="s0upd" + sfx)
        p.op("scalar", lambda s_: s_.copy(out=S0bf[:, 512 * s:512 * (s + 1)],
             in_=S0[:, 512 * s:512 * (s + 1)]), reads=[gb("S0")],
             writes=[gb("S0bf")])

        expG_b = (expG[:, :].rearrange("p (j b) -> p b j", b=2)
                  .unsqueeze(3).broadcast_to([128, 2, 8, 64]))
        p.op("vector", lambda v: v.tensor_tensor(
             out=tmpO[:, :].rearrange("p (b j d) -> p b j d", b=2, j=8),
             in0=PP[1][:, :].rearrange("p (b j d) -> p b j d", b=2, j=8),
             in1=expG_b, op=ALU.mult),
             reads=[pb[2], pb[3], gb("expG")], writes=[gb("tmpO")],
             tag="tt1" + sfx)
        p.op("vector", lambda v: v.tensor_tensor(
             out=attn[:, :].rearrange("p (j b d) -> p j b d", b=2, d=64),
             in0=tmpO[:, :].rearrange("p (b j d) -> p j b d", b=2, j=8),
             in1=PP[0][:, :].rearrange("p (j b d) -> p j b d", b=2, d=64),
             op=ALU.add),
             reads=[gb("tmpO"), pb[0], pb[1]], writes=[gb("attn")],
             tag="tt2" + sfx)
        sm1 = SMS[1]
        bns1 = sm1[:, 6:18]
        p.op("vector", lambda v: v.bn_stats(out=bns1[:, 0:6], in_=attn[:, 0:512]),
             reads=[gb("attn")], writes=[gb("bns1")])
        p.op("vector", lambda v: v.bn_stats(out=bns1[:, 6:12],
             in_=attn[:, 512:1024]), reads=[gb("attn")], writes=[gb("bns1")])
        p.op("vector", lambda v: v.bn_aggr(out=sm1[:, 4:6], in_=bns1[:, :]),
             reads=[gb("bns1")], writes=[gb("mv1")])
        p.op("vector", lambda v: v.tensor_scalar(out=sm1[:, 3:4],
             in0=sm1[:, 4:5], scalar1=sm1[:, 4:5], scalar2=sm1[:, 5:6],
             op0=ALU.mult, op1=ALU.add), reads=[gb("mv1")], writes=[gb("ms1")])
        if handoff is not None:
            handoff(1)
        p.op("scalar", lambda s_: s_.activation(out=sm1[:, 2:3], in_=sm1[:, 3:4],
             func=AF.Ln, bias=RMS_EPS, scale=1.0), reads=[gb("ms1")],
             writes=[gb("lv1")])
        p.op("scalar", lambda s_: s_.activation(out=sm1[:, 0:1], in_=sm1[:, 2:3],
             func=AF.Exp, bias=0.0, scale=-0.5), reads=[gb("lv1")],
             writes=[gb("rstd1")])
        p.op("vector", lambda v: v.scalar_tensor_tensor(out=tmpO[:, :],
             in0=thog[:, :], scalar=1.0, in1=attn[:, :], op0=ALU.add,
             op1=ALU.mult), reads=[gb("thog"), gb("attn"), gb("tmpO")],
             writes=[gb("tmpO")], tag="ymul0" + sfx)
        if handoff is not None:
            handoff(2)

        p.op("vector", lambda v: v.tensor_scalar(out=y_bf[:, :],
             in0=tmpO[:, :], scalar1=sm1[:, 0:1], scalar2=None, op0=ALU.mult),
             reads=[gb("tmpO"), gb("rstd1"), gb("thxg")],
             writes=[gb("thxg")], tag="ymul" + sfx)
        if handoff is not None:
            handoff(3)

        def y_tp(t):
            last = None
            for k in range(8):
                last = t.transpose(
                    PP[3][:, 0:512].bitcast(BF16)[:, 128 * k:128 * (k + 1)],
                    y_bf[:, 128 * k:128 * (k + 1)], t_id[:, :])
            return last
        p.op("tensor", y_tp, reads=[gb("thxg"), gb("id"), gb("kt")],
             writes=[pb[6]])
        p.op("vector", lambda v: v.tensor_copy(out=yT[:, :],
             in_=PP[3][:, 0:512].bitcast(BF16)), reads=[pb[6]], writes=[gb("kt")])
        yTv = yT.rearrange("p (k t) -> p k t", k=8)

        for hf in range(2):
            def womm_h(t, hf=hf):
                last = None
                for k in range(8):
                    last = t.matmul(PP[3][:, 0:512], yTv[:, k, :],
                                    WOT[:, k, 512 * hf:512 * (hf + 1)],
                                    start=(k == 0), stop=(k == 7))
                return last
            p.op("tensor", womm_h, reads=[gb("kt"), bw["o"]], writes=[pb[6]],
                 tag=f"womm{hf}" + sfx)
            p.op("vector", lambda v, hf=hf: v.tensor_tensor(
                 out=x1_t[:, 512 * hf:512 * (hf + 1)],
                 in0=xi[:, 512 * hf:512 * (hf + 1)],
                 in1=PP[3][:, 0:512], op=ALU.add), reads=[bxi, pb[6]],
                 writes=[gb("x1")], tag=f"resid{hf}" + sfx)
        p.dma("sync", out=x1_s[row0:row0 + CH, :], in_=x1_t[:, :],
              reads=[gb("x1")], writes=[gb("x1s")], tag="x1out" + sfx)

        sm2 = SMS[2]
        bns2 = sm2[:, 6:18]
        p.op("vector", lambda v: v.bn_stats(out=bns2[:, 0:6], in_=x1_t[:, 0:512]),
             reads=[gb("x1")], writes=[gb("bns2")])
        p.op("vector", lambda v: v.bn_stats(out=bns2[:, 6:12],
             in_=x1_t[:, 512:1024]), reads=[gb("x1")], writes=[gb("bns2")])
        p.op("vector", lambda v: v.bn_aggr(out=sm2[:, 4:6], in_=bns2[:, :]),
             reads=[gb("bns2")], writes=[gb("mv2")])
        p.op("scalar", lambda s_: s_.activation(out=sm2[:, 2:3], in_=sm2[:, 5:6],
             func=AF.Ln, bias=LN_EPS, scale=1.0), reads=[gb("mv2")],
             writes=[gb("lv2")])
        p.op("scalar", lambda s_: s_.activation(out=sm2[:, 0:1], in_=sm2[:, 2:3],
             func=AF.Exp, bias=0.0, scale=-0.5), reads=[gb("lv2")],
             writes=[gb("rstd2")])
        p.op("vector", lambda v: v.tensor_scalar(out=sm2[:, 1:2],
             in0=sm2[:, 4:5], scalar1=sm2[:, 0:1], scalar2=-1.0, op0=ALU.mult,
             op1=ALU.mult), reads=[gb("mv2"), gb("rstd2")], writes=[gb("negmr2")])
        p.op("vector", lambda v: v.tensor_scalar(out=ln2[:, :], in0=x1_t[:, :],
             scalar1=sm2[:, 0:1], scalar2=sm2[:, 1:2], op0=ALU.mult,
             op1=ALU.add),
             reads=[gb("x1"), gb("rstd2"), gb("negmr2"), gb("v")],
             writes=[gb("v")])
        p.op("vector", lambda v, idx=idx: v.tensor_reduce(
             out=am64[:, idx:idx + 1], in_=ln2[:, :], axis=AX.X, op=ALU.max,
             apply_absolute_value=True), reads=[gb("v")], writes=[gb("am64")])

        def ln2_tp(t):
            last = None
            for k in range(8):
                last = t.transpose(
                    PP[3][:, 0:512].bitcast(BF16)[:, 128 * k:128 * (k + 1)],
                    ln2[:, 128 * k:128 * (k + 1)], t_id[:, :])
            return last
        p.op("tensor", ln2_tp, reads=[gb("v"), gb("id")], writes=[pb[6]])
        p.op("vector", lambda v: v.tensor_copy(out=lnT2[:, :],
             in_=PP[3][:, 0:512].bitcast(BF16)), reads=[pb[6]],
             writes=[gb("lnT2b")])
        tok0 = s * S + c * CH
        p.dma("scalar",
              out=ln2t_s.rearrange("k p t -> p k t")[:, :, tok0:tok0 + CH],
              in_=lnT2.rearrange("p (k t) -> p k t", k=8),
              reads=[gb("lnT2b")], writes=[gb("ln2ts")], tag="ln2out" + sfx)
        if post is not None:
            post()

    order = [(s, c) for c in range(NC_) for s in range(n_seq)]
    for i in range(min(2, len(order))):
        s0, c0 = order[i]
        r0 = s0 * S + c0 * CH
        p.dma("sync", out=x_t[i % 2][:, :], in_=x_in[r0:r0 + CH, :],
              writes=[bx[i % 2]])

    ln1_stats(*order[0], 0)
    ln1_apply(*order[0], 0)
    emit_lnT(0)
    emit_ld(0)
    ldg_front(0)
    emit_qkg(0)
    ldg_back(0)
    expgcp_emit(0)
    rmm_g(1, 0)
    scan_g(1, 0)
    bw["v"] = prep_weight("v", Wv, o_wt["v"])
    bw["og"] = prep_weight("og", Wog, o_wt["og"])
    bw["o"] = prep_weight("o", Wo, o_wot, scale_col=True)
    emit_stfence()
    emit_fdma()
    for idx, (s0, c0) in enumerate(order):
        def pre(phase, idx=idx):
            if idx + 1 < len(order):
                if phase == 1:
                    ln1_stats(*order[idx + 1], idx + 1)
                else:
                    ln1_apply(*order[idx + 1], idx + 1)

        def handoff(phase, idx=idx):
            nxt = idx + 1 < len(order)
            if phase == 1:
                if nxt:
                    emit_lnT(idx + 1)
                return
            if phase == 2:
                lim = acb_ct[0] if idx >= len(order) - 2 else acb_ct[0] - 1
                if back_ct[0] < lim:
                    emit_back()
                if nxt:
                    emit_ld(idx + 1)
                while back_ct[0] < lim:
                    emit_back()
                if nxt:
                    emit_qkg(idx + 1)
                return
            if nxt:
                ldg_front(idx + 1)
            fslot((idx + 1) * NU // NTOT + 1)
            if nxt:
                ldg_back(idx + 1)

        def post(idx=idx):
            if idx + 1 < len(order):
                expgcp_emit(idx + 1)
                rmm_g(1, idx + 1)
                scan_g(1, idx + 1)
        chunk(s0, c0, idx, handoff=handoff, pre=pre, post=post)
        if idx + 2 < len(order):
            s2, c2 = order[idx + 2]
            r2 = s2 * S + c2 * CH
            p.dma("sync", out=x_t[(idx + 2) % 2][:, :], in_=x_in[r2:r2 + CH, :],
                  writes=[bx[(idx + 2) % 2]], tag=f"xld{idx + 2}")
    while back_ct[0] < NU:
        emit_fdma()
        emit_fa()
        emit_fb()
        if back_ct[0] < acb_ct[0]:
            emit_back()

    if stage <= 1:
        p.build()
        return nc

    def cross_part_max2(src, bsrcs, cc_i, width, tagn):
        p.op("tensor", lambda t: t.transpose(PP[3][0:width, 512:640], src,
             t_idf[:, :]), reads=list(bsrcs) + [gb("idf")], writes=[pb[7]],
             tag=tagn + "tp")
        p.op("vector", lambda v: v.tensor_reduce(
             out=AFA[0:width, o_sc + 62:o_sc + 63],
             in_=PP[3][0:width, 512:640], axis=AX.X, op=ALU.max),
             reads=[], writes=[pb[7], gb("ccv" + tagn)], tag=tagn + "red")
        p.dma("gpsimd", out=cc_i.ap()[0:1, 0:width],
              in_=AFA[0:width, o_sc + 62:o_sc + 63],
              reads=[gb("ccv" + tagn)], writes=[gb("ccin" + tagn)],
              tag=tagn + "st")

    def allreduce(i, width, tagn):
        def cc(g, i=i):
            return g.collective_compute(
                "AllReduce", ALU.max, replica_groups=[list(range(n_cores))],
                ins=[cc_in[i].ap()[0:1, 0:width].opt()],
                outs=[cc_out[i].ap()[0:1, 0:width].opt()])
        p.op("gpsimd", cc, reads=[gb("ccin" + tagn)], writes=[gb("ccout" + tagn)],
             tag=tagn + "ar")
        dst = AFA[:, o_sc + 8 * i:o_sc + 8 * i + width]
        p.dma("gpsimd", out=dst,
              in_=cc_out[i].ap()[0:1, 0:width].partition_broadcast(128).squeeze(1),
              reads=[gb("ccout" + tagn)], writes=[gb("gmax" + tagn)],
              tag=tagn + "rd")
        return dst

    wspan = [abv(o_ln, 4096), abv(o_ln + 4096, 4096)]
    bwsp = [gb(f"wsp{j}") for j in range(2)]
    p.op("vector", lambda v: v.memset(AFA[0:1, o_sc + 7:o_sc + 8], 0.0),
         reads=[],
         writes=[gb("ln"), gb("lnT0"), gb("q"), gb("thxg"), gb("kt"),
                 gb("v"), gb("thog"), gb("kT")] + bwsp, tag="wspfence")

    amfin = AFA[:, o_am + 64:o_am + 65]
    p.op("vector", lambda v: v.tensor_reduce(out=amfin, in_=am64[:, :],
         axis=AX.X, op=ALU.max), reads=[gb("am64")], writes=[gb("amfin")])
    cross_part_max2(amfin, [gb("amfin")], cc_in[0], 1, "ar1")
    g1 = allreduce(0, 1, "ar1")
    s1 = AFA[:, o_sc + 32:o_sc + 33]
    r1 = AFA[:, o_sc + 33:o_sc + 34]
    p.op("vector", lambda v: v.tensor_scalar(out=s1, in0=g1, scalar1=1.0 / QMAX,
         scalar2=1e-12, op0=ALU.mult, op1=ALU.max), reads=[gb("gmaxar1")],
         writes=[gb("s1")])
    p.op("vector", lambda v: v.reciprocal(out=r1, in_=s1), reads=[gb("s1")],
         writes=[gb("r1")])

    xq = abv(o_wt["q"], 8 * NT)
    bxq = gb("xq")
    old_wbufs = [bw["q"], bw["k"], bw["v"], bw["g"]]
    p.dma("sync", out=xq.rearrange("p (k t) -> p k t", k=8),
          in_=ln2t_s.rearrange("k p t -> p k t"),
          reads=[gb("ln2ts")], writes=old_wbufs + [bxq], tag="xqld")
    i16tmp = abv(o_stage, 4096, I16)
    p.op("vector", lambda v: v.memset(AFA[0:1, o_sc + 1:o_sc + 2], 0.0),
         reads=[gb("wqi0"), gb("wqi1"), gb("wdq0"), gb("wdq1"), gb("wdq2"),
                gb("wev0"), gb("wev1"),
                gb("stage0"), gb("lnT1")],
         writes=[gb("i16t"), gb("i16u")], tag="b1fence")
    xqv = xq.rearrange("p (k t) -> p k t", k=8)
    i16tmp2 = abv(o_stage + 6144, 2048, I16)
    for k in range(8):
        for off in range(0, NT, 2048):
            w = min(2048, NT - off)
            sl = xqv[:, k, off:off + w]
            if k % 2 == 0:
                p.op("vector", lambda v, sl=sl, w=w: v.tensor_scalar(
                     out=i16tmp[:, 0:w], in0=sl, scalar1=r1, scalar2=None,
                     op0=ALU.mult), reads=[bxq, gb("r1")], writes=[gb("i16t")],
                     tag=f"xqr{k}{off}")
                p.op("vector", lambda v, sl=sl, w=w: v.tensor_copy(
                     out=sl, in_=i16tmp[:, 0:w]), reads=[gb("i16t")],
                     writes=[bxq], tag=f"xqc{k}{off}")
            else:
                p.op("scalar", lambda s_, sl=sl, w=w: s_.activation(
                     out=i16tmp2[:, 0:w], in_=sl, func=AF.Identity, bias=0.0,
                     scale=r1), reads=[bxq, gb("r1")], writes=[gb("i16u")],
                     tag=f"xqr{k}{off}")
                p.op("scalar", lambda s_, sl=sl, w=w: s_.copy(
                     out=sl, in_=i16tmp2[:, 0:w]), reads=[gb("i16u")],
                     writes=[bxq], tag=f"xqc{k}{off}")

    if stage <= 2:
        p.build()
        return nc

    wtc = [abv(o_stage + 1024 * j, 1024) for j in range(4)]
    bwtc = [gb(f"wtc{j}") for j in range(4)]
    p.op("vector", lambda v: v.memset(AFA[0:1, o_sc + 6:o_sc + 7], 0.0),
         reads=[gb("i16t")], writes=[gb("wtc0"), gb("wtc1"), gb("wtc2"),
                                     gb("wtc3")], tag="b1fence2")
    b1s = [abv(OB1S + NT * j, NT) for j in range(4)]
    bb1s = [gb(f"b1s{j}") for j in range(4)]
    p.op("vector", lambda v: v.memset(AFA[0:1, o_sc + 2:o_sc + 3], 0.0),
         reads=[bw["og"], bw["o"]], writes=bb1s, tag="b1sfence")
    amU24 = am64[:, 0:24]
    amG24 = am64[:, 24:48]
    upt_v = upt_s.rearrange("i p t -> p i t")
    gat_v = gat_s.rearrange("i p t -> p i t")

    def b1_block(i, wsrc_s, bsrc, slot, dst_v, bdst, amcol, tagn):
        wv = wtc[slot].rearrange("p (k o) -> p k o", k=8)
        p.dma("sync", out=wtc[slot][:, :], in_=wsrc_s[i], reads=[bsrc],
              writes=[bwtc[slot]], tag=f"b1w{tagn}{i}", sem_group=f"wtc{slot}")
        strip = b1s[slot]
        bstrip = bb1s[slot]
        ngrp = max(1, TT1 // 4)
        for tg in range(ngrp):
            nt4 = min(4, TT1 - 4 * tg)
            ppA = 2 * (tg % 2)
            pbs = [pb[2 * ppA + j] for j in range(2 * nt4 // 2 + (nt4 % 2))]
            pbs = [pb[2 * ppA + j] for j in range(nt4)]

            def mm(t, wv=wv, tg=tg, nt4=nt4, ppA=ppA):
                last = None
                for k in range(8):
                    for t4 in range(nt4):
                        tt = 4 * tg + t4
                        pt = PP[ppA + t4 // 2][:, 512 * (t4 % 2):
                                               512 * (t4 % 2) + 512]
                        last = t.matmul(pt, wv[:, k, :],
                                        xqv[:, k, 512 * tt:512 * (tt + 1)],
                                        start=(k == 0), stop=(k == 7))
                return last
            p.op("tensor", mm, reads=[bwtc[slot], bxq], writes=pbs,
                 tag=f"b1{tagn}mm{i}_{tg}")
            for t4 in range(nt4):
                tt = 4 * tg + t4
                pt = PP[ppA + t4 // 2][:, 512 * (t4 % 2):512 * (t4 % 2) + 512]
                p.op("scalar", lambda s_, pt=pt, tt=tt, strip=strip:
                     s_.activation(out=strip[:, 512 * tt:512 * (tt + 1)],
                                   in_=pt, func=AF.Identity, bias=0.0,
                                   scale=s1),
                     reads=[pb[2 * ppA + t4], gb("s1")], writes=[bstrip],
                     tag=f"b1{tagn}ev{i}_{tt}")
        p.op("vector", lambda v, strip=strip, amcol=amcol: v.tensor_reduce(
             out=amcol, in_=strip[:, :], axis=AX.X, op=ALU.max,
             apply_absolute_value=True), reads=[bstrip], writes=[gb("am64")],
             tag=f"b1{tagn}am{i}")
        p.dma("scalar", out=dst_v[:, i, :], in_=strip[:, :], reads=[bstrip],
              writes=[bdst], tag=f"b1{tagn}st{i}")

    for i in range(NIC):
        b1_block(i, wut_s, gb("wuts"), i % 2, upt_v, gb("upts"),
                 amU24[:, i:i + 1], "u")
        b1_block(i, wgt_s, gb("wgts"), 2 + i % 2, gat_v, gb("gats"),
                 amG24[:, i:i + 1], "g")
        if i in (NIC - 2, NIC - 1):
            j = i - (NIC - 2)
            p.dma("sync", out=wspan[j][:, 0:NT], in_=gat_v[:, j, :],
                  reads=[gb("gats")], writes=[bwsp[j]], tag=f"b2pre{j}",
                  sem_group=f"b2p{j}")

    if stage <= 3:
        p.build()
        return nc

    amUG = AFA[:, o_sc + 40:o_sc + 42]
    p.op("vector", lambda v: v.tensor_reduce(out=amUG[:, 0:1], in_=amU24,
         axis=AX.X, op=ALU.max), reads=[gb("am64")], writes=[gb("amUf")])
    p.op("vector", lambda v: v.tensor_reduce(out=amUG[:, 1:2], in_=amG24,
         axis=AX.X, op=ALU.max), reads=[gb("am64")], writes=[gb("amGf")])
    cross_part_max2(amUG, [gb("amUf"), gb("amGf")], cc_in[1], 2, "ar2")
    g2 = allreduce(1, 2, "ar2")
    s2 = AFA[:, o_sc + 34:o_sc + 35]
    r2 = AFA[:, o_sc + 35:o_sc + 36]
    sg = AFA[:, o_sc + 36:o_sc + 37]
    csig = AFA[:, o_sc + 37:o_sc + 38]
    rg = AFA[:, o_sc + 49:o_sc + 50]
    p.op("vector", lambda v: v.tensor_scalar(out=s2, in0=g2[:, 0:1],
         scalar1=1.0 / QMAX, scalar2=1e-12, op0=ALU.mult, op1=ALU.max),
         reads=[gb("gmaxar2")], writes=[gb("s2")])
    p.op("vector", lambda v: v.reciprocal(out=r2, in_=s2), reads=[gb("s2")],
         writes=[gb("r2")])
    p.op("vector", lambda v: v.tensor_scalar(out=sg, in0=g2[:, 1:2],
         scalar1=1.0 / QMAX, scalar2=1e-12, op0=ALU.mult, op1=ALU.max),
         reads=[gb("gmaxar2")], writes=[gb("sg")])
    p.op("vector", lambda v: v.tensor_scalar(out=csig, in0=sg, scalar1=0.5,
         scalar2=None, op0=ALU.mult), reads=[gb("sg")], writes=[gb("csig")])
    p.op("vector", lambda v: v.reciprocal(out=rg, in_=sg), reads=[gb("sg")],
         writes=[gb("rg")])

    WDN = abv(OB_WDN, 24 * 1024).rearrange("p (i r) -> p i r", i=24)
    bwdn = gb("wdnsb")
    gs = [abv(OB2S + NT * j, NT) for j in range(10)]
    bgs = [gb(f"gs{j}") for j in range(10)]
    def emit_wdn():
        p.op("vector", lambda v: v.memset(AFA[0:1, o_sc + 3:o_sc + 4], 0.0),
             reads=[bw["og"], bw["o"], gb("wdt"), gb("i16t"),
                    bwsp[0], bwsp[1], gb("i16u"),
                    gb("wdq0"), gb("wdq1"), gb("wdq2"), gb("wev0"), gb("wev1"),
                    gb("wtc0"), gb("wtc1"), gb("wtc2"), gb("wtc3"),
                    gb("b1s0"), gb("b1s1"), gb("b1s2"), gb("b1s3"),
                    gb("ln"), gb("lnT0"), gb("lnT1"), gb("q"), gb("thxg"),
                    gb("kt"), gb("v"),
                    gb("thog"), gb("kT"), gb("qT"), gb("Dt"), gb("At"),
                    gb("k2"), gb("lnT2b"), bxq],
             writes=[bwdn] + bgs, tag="wdnfence")
        p.dma("sync", out=WDN, in_=wdnt_s.rearrange("i p d -> p i d"),
              reads=[gb("wdns")], writes=[bwdn], tag="wdnld")

    amT_cols = am64[:, 0:24]
    amO_cols = am64[:, 24:48]

    def b2_head(i):
        if i < 2:
            gl, bgl = wspan[i][:, 0:NT], bwsp[i]
        else:
            gl, bgl = gs[i % 2], bgs[i % 2]
        gi = gs[2 + i % 2].bitcast(I16)
        bgi = bgs[2 + i % 2]
        th = gs[4 + i % 2]
        bth = bgs[4 + i % 2]
        sig = gs[6 + i % 2].bitcast(U16)
        bsig = bgs[6 + i % 2]
        if i >= 2:
            p.dma("sync", out=gl[:, :], in_=gat_v[:, i, :], reads=[gb("gats")],
                  writes=[bgl], tag=f"b2ld{i}", sem_group=f"b2g{i % 2}")
        p.op("scalar", lambda s_, gl=gl, gi=gi: s_.activation(out=gi[:, :],
             in_=gl[:, :], func=AF.Identity, bias=0.0, scale=rg),
             reads=[bgl, gb("rg")], writes=[bgi], tag=f"b2gi{i}")
        p.op("scalar", lambda s_, gi=gi, th=th: s_.activation(out=th[:, :],
             in_=gi[:, :], func=AF.Tanh, bias=0.0, scale=csig),
             reads=[bgi, gb("csig")], writes=[bth], tag=f"b2th{i}")
        p.op("gpsimd", lambda g, th=th, sig=sig: g.tensor_scalar(out=sig[:, :],
             in0=th[:, :], scalar1=32767.5, scalar2=32768.0, op0=ALU.mult,
             op1=ALU.add), reads=[bth], writes=[bsig], tag=f"b2sig{i}")

    def b2_tail(i):
        gi = gs[2 + i % 2].bitcast(I16)
        bgi = bgs[2 + i % 2]
        sig = gs[6 + i % 2].bitcast(U16)
        bsig = bgs[6 + i % 2]
        tout = gs[8 + i % 2]
        bto = bgs[8 + i % 2]
        p.op("vector", lambda v, gi=gi, sig=sig, tout=tout: v.tensor_tensor(
             out=tout[:, :], in0=gi[:, :], in1=sig[:, :], op=ALU.mult),
             reads=[bgi, bsig], writes=[bto], tag=f"b2t{i}")
        p.op("vector", lambda v, tout=tout, i=i: v.tensor_reduce(
             out=amT_cols[:, i:i + 1], in_=tout[:, :], axis=AX.X, op=ALU.max,
             apply_absolute_value=True), reads=[bto], writes=[gb("am64")],
             tag=f"b2am{i}")
        p.dma("scalar", out=tra_s.rearrange("i p t -> p i t")[:, i, :],
              in_=tout[:, :], reads=[bto], writes=[gb("tras")], tag=f"b2st{i}")

    b2_head(0)
    b2_head(1)
    emit_wdn()
    b2_tail(0)
    for i in range(2, NIC):
        b2_head(i)
        b2_tail(i - 1)
    b2_tail(NIC - 1)

    if stage <= 4:
        p.build()
        return nc

    amT = AFA[:, o_sc + 43:o_sc + 44]
    p.op("vector", lambda v: v.tensor_reduce(out=amT, in_=amT_cols, axis=AX.X,
         op=ALU.max), reads=[gb("am64")], writes=[gb("amTf")])
    cross_part_max2(amT, [gb("amTf")], cc_in[2], 1, "ar3")
    g3 = allreduce(2, 1, "ar3")
    s3 = AFA[:, o_sc + 38:o_sc + 39]
    c3 = AFA[:, o_sc + 39:o_sc + 40]
    tmp1 = AFA[:, o_sc + 44:o_sc + 45]
    p.op("vector", lambda v: v.tensor_scalar(out=tmp1, in0=g3, scalar1=sg,
         scalar2=1.0 / 65536.0 / QMAX, op0=ALU.mult, op1=ALU.mult),
         reads=[gb("gmaxar3"), gb("sg")], writes=[gb("tmp1")])
    p.op("vector", lambda v: v.tensor_scalar(out=s3, in0=tmp1, scalar1=1e-12,
         scalar2=None, op0=ALU.max), reads=[gb("tmp1")], writes=[gb("s3")])
    p.op("vector", lambda v: v.reciprocal(out=c3, in_=s3), reads=[gb("s3")],
         writes=[gb("c3")])
    p.op("vector", lambda v: v.tensor_scalar(out=c3, in0=c3, scalar1=sg,
         scalar2=1.0 / 65536.0, op0=ALU.mult, op1=ALU.mult),
         reads=[gb("c3"), gb("sg")], writes=[gb("c3")])

    def b3_head(i):
        tl, btl = gs[i % 2], bgs[i % 2]
        ul, bul = gs[2 + i % 2], bgs[2 + i % 2]
        g2i = gs[4 + i % 2].bitcast(I16)
        bg2 = bgs[4 + i % 2]
        ui = gs[6 + i % 2].bitcast(I16)
        bui = bgs[6 + i % 2]
        p.dma("sync", out=tl[:, :],
              in_=tra_s.rearrange("i p t -> p i t")[:, i, :],
              reads=[gb("tras")], writes=[btl], tag=f"b3ldt{i}",
              sem_group=f"b3t{i % 2}")
        p.dma("gpsimd", out=ul[:, :], in_=upt_v[:, i, :], reads=[gb("upts")],
              writes=[bul], tag=f"b3ldu{i}", sem_group=f"b3u{i % 2}")
        p.op("scalar", lambda s_, tl=tl, g2i=g2i: s_.activation(
             out=g2i[:, :], in_=tl[:, :], func=AF.Identity, bias=0.0,
             scale=c3), reads=[btl, gb("c3")], writes=[bg2], tag=f"b3g2{i}")
        p.op("scalar", lambda s_, ul=ul, ui=ui: s_.activation(
             out=ui[:, :], in_=ul[:, :], func=AF.Identity, bias=0.0,
             scale=r2), reads=[bul, gb("r2")], writes=[bui], tag=f"b3ui{i}")

    def b3_tail(i):
        g2i = gs[4 + i % 2].bitcast(I16)
        bg2 = bgs[4 + i % 2]
        ui = gs[6 + i % 2].bitcast(I16)
        bui = bgs[6 + i % 2]
        oout = gs[8 + i % 2]
        bo = bgs[8 + i % 2]
        p.op("vector", lambda v, g2i=g2i, ui=ui, oout=oout: v.tensor_tensor(
             out=oout[:, :], in0=g2i[:, :], in1=ui[:, :], op=ALU.mult),
             reads=[bg2, bui], writes=[bo], tag=f"b3o{i}")
        p.op("vector", lambda v, oout=oout, i=i: v.tensor_reduce(
             out=amO_cols[:, i:i + 1], in_=oout[:, :], axis=AX.X, op=ALU.max,
             apply_absolute_value=True), reads=[bo], writes=[gb("am64")],
             tag=f"b3am{i}")
        p.dma("scalar", out=ora_s.rearrange("i p t -> p i t")[:, i, :],
              in_=oout[:, :], reads=[bo], writes=[gb("oras")], tag=f"b3st{i}")

    for i in range(NIC):
        b3_head(i)
        if i >= 1:
            b3_tail(i - 1)
    b3_tail(NIC - 1)

    if stage <= 5:
        p.build()
        return nc

    amO = AFA[:, o_sc + 45:o_sc + 46]
    p.op("vector", lambda v: v.tensor_reduce(out=amO, in_=amO_cols, axis=AX.X,
         op=ALU.max), reads=[gb("am64")], writes=[gb("amOf")])
    cross_part_max2(amO, [gb("amOf")], cc_in[3], 1, "ar4")
    g4 = allreduce(3, 1, "ar4")
    s4 = AFA[:, o_sc + 46:o_sc + 47]
    c4 = AFA[:, o_sc + 47:o_sc + 48]
    s32 = AFA[:, o_sc + 48:o_sc + 49]
    p.op("vector", lambda v: v.tensor_scalar(out=s32, in0=s3, scalar1=s2,
         scalar2=None, op0=ALU.mult), reads=[gb("s3"), gb("s2")],
         writes=[gb("s32")])
    p.op("vector", lambda v: v.tensor_scalar(out=s4, in0=g4, scalar1=s32,
         scalar2=1.0 / QMAX, op0=ALU.mult, op1=ALU.mult),
         reads=[gb("gmaxar4"), gb("s32")], writes=[gb("s4")])
    p.op("vector", lambda v: v.tensor_scalar(out=s4, in0=s4, scalar1=1e-12,
         scalar2=None, op0=ALU.max), reads=[gb("s4")], writes=[gb("s4")])
    p.op("vector", lambda v: v.reciprocal(out=c4, in_=s4), reads=[gb("s4")],
         writes=[gb("c4")])
    p.op("vector", lambda v: v.tensor_scalar(out=c4, in0=c4, scalar1=s32,
         scalar2=None, op0=ALU.mult), reads=[gb("c4"), gb("s32")],
         writes=[gb("c4")])

    NG = NT // 512
    xo2 = [abv(OB_XO + 12288 * j, 24 * 512) for j in range(2)]
    bxo2 = [gb("xo0"), gb("xo1")]
    x1l = [afv(o_x[0], 1024), afv(o_x[1], 1024)]
    bx1l = [gb("x1l0"), gb("x1l1")]
    outt = [afv(o_attn, 1024), afv(o_tmpO, 1024)]
    bout = [gb("bo0"), gb("bo1")]
    p.op("vector", lambda v: v.memset(AFA[0:1, o_sc + 5:o_sc + 6], 0.0),
         reads=[gb("gs2"), gb("gs3"), gb("gs4"), gb("gs5"), gb("gs6"),
                gb("gs7"), gb("gs8"), gb("gs9")],
         writes=[bxo2[0], bxo2[1], gb("attn"), gb("tmpO"), gb("x1"),
                 bx[0], bx[1], bout[0], bout[1], bx1l[0], bx1l[1]],
         tag="xofence")

    def b4_conv(gi_):
        t0 = 512 * gi_
        xov = xo2[gi_ % 2].rearrange("p (i t) -> p i t", i=24)
        bxo = bxo2[gi_ % 2]
        for i in range(NIC):
            sl4 = (0, 1, 8, 9)[i % 4]
            ol, bol = gs[sl4], bgs[sl4]
            p.dma("gpsimd", out=ol[:, 0:512],
                  in_=ora_s.rearrange("i p t -> p i t")[:, i, t0:t0 + 512],
                  reads=[gb("oras")], writes=[bol], tag=f"b4ld{gi_}_{i}",
                  sem_group=f"b4o{i % 4}")
            p.op("vector", lambda v, ol=ol: v.tensor_scalar(
                 out=ol[:, 0:512].bitcast(I16), in0=ol[:, 0:512], scalar1=c4,
                 scalar2=None, op0=ALU.mult), reads=[bol, gb("c4")],
                 writes=[bol], tag=f"b4xi{gi_}_{i}")
            p.op("vector", lambda v, ol=ol, i=i, xov=xov: v.tensor_copy(
                 out=xov[:, i, :], in_=ol[:, 0:512].bitcast(I16)),
                 reads=[bol], writes=[bxo], tag=f"b4xb{gi_}_{i}")

    def b4_mms(gi_):
        t0 = 512 * gi_
        xov = xo2[gi_ % 2].rearrange("p (i t) -> p i t", i=24)
        bxo = bxo2[gi_ % 2]
        for tp_ in range(2):
            wi = tp_ % 2
            pA = 2 * wi
            for h_ in range(2):
                tok = t0 + 128 * (2 * tp_ + h_)
                p.dma("sync", out=x1l[h_][:, :], in_=x1_s[tok:tok + CH, :],
                      reads=[gb("x1s")], writes=[bx1l[h_]],
                      tag=f"b4x1{gi_}_{tp_}_{h_}", sem_group=f"b4x1{h_}")
            pbd = [pb[2 * pA], pb[2 * pA + 1], pb[2 * pA + 2], pb[2 * pA + 3]]

            def dmm(t, tp_=tp_, pA=pA, xov=xov):
                last = None
                for i in range(NIC):
                    st, sp_ = (i == 0), (i == NIC - 1)
                    for h_ in range(2):
                        tt = 2 * tp_ + h_
                        t.matmul(PP[pA + h_][:, 0:512],
                                 xov[:, i, 128 * tt:128 * (tt + 1)],
                                 WDN[:, i, 0:512], start=st, stop=sp_)
                        last = t.matmul(PP[pA + h_][:, 512:1024],
                                        xov[:, i, 128 * tt:128 * (tt + 1)],
                                        WDN[:, i, 512:1024], start=st, stop=sp_)
                return last
            p.op("tensor", dmm, reads=[bxo, bwdn], writes=pbd,
                 tag=f"b4mm{gi_}_{tp_}")
            for h_ in range(2):
                tok = t0 + 128 * (2 * tp_ + h_)
                ppd = PP[pA + h_]
                p.op("vector", lambda v, h_=h_, ppd=ppd: v.scalar_tensor_tensor(
                     out=outt[h_][:, :], in0=ppd[:, :], scalar=s4,
                     in1=x1l[h_][:, :], op0=ALU.mult, op1=ALU.add),
                     reads=[pb[2 * (pA + h_)], pb[2 * (pA + h_) + 1],
                            bx1l[h_], gb("s4")], writes=[bout[h_]],
                     tag=f"b4ev{gi_}_{tp_}_{h_}")
                p.dma("sync", out=out_d[tok:tok + CH, :], in_=outt[h_][:, :],
                      reads=[bout[h_]], writes=[gb("outd")],
                      tag=f"b4st{gi_}_{tp_}_{h_}")

    for gi_ in range(NG):
        b4_conv(gi_)
        if gi_ >= 1:
            b4_mms(gi_ - 1)
    b4_mms(NG - 1)

    p.build()
    return nc


def _consts():
    idbf = np.eye(128, dtype=BF)
    idf32 = np.eye(128, dtype=np.float32)
    idt = np.tile(np.eye(128, dtype=np.float32), (1, 8)).astype(BF)
    sel = np.zeros((2, 16, 1024), dtype=BF)
    for g in range(2):
        for j in range(8):
            sel[g, 2 * j + g, 128 * j:128 * (j + 1)] = 1
    ones16 = np.ones((16, 128), dtype=np.float32)
    pm = np.zeros((16, 8), dtype=np.float32)
    for k in range(16):
        pm[k, k // 2] = 1
    eo = np.zeros((16, 128), dtype=np.float32)
    for k in range(16):
        for m in range(128):
            if (k % 2) == (1 if m >= 64 else 0):
                eo[k, m] = 1
    aconst = np.array([[1e-5, float(np.finfo(np.float32).eps), 1.0, -1.0,
                        0.5, -0.5, 0.0, 1.0 / 65536.0]], dtype=np.float32)
    return dict(idbf=idbf, idf32=idf32, idt=idt, sel=sel, ones16=ones16,
                pmask=pm, eomask=eo, aconst=aconst)


def kernel(L0_attn_x, L0_attn_Wq, L0_attn_Wk, L0_attn_Wv, L0_attn_Wo,
           L0_attn_Wg, L0_attn_Wog, L0_attn_Wd, L0_attn_bd, L0_attn_norm_w,
           L0_ffn_w_gate, L0_ffn_w_up, L0_ffn_w_down):
    x = np.ascontiguousarray(np.asarray(L0_attn_x, dtype=np.float32))
    Bt, S, _ = x.shape
    n_cores = 8
    per = Bt // n_cores
    nc = build_program(n_seq=per, S=S, n_cores=n_cores)
    consts = _consts()
    common = dict(
        Wq=np.ascontiguousarray(L0_attn_Wq, np.float32),
        Wk=np.ascontiguousarray(L0_attn_Wk, np.float32),
        Wv=np.ascontiguousarray(L0_attn_Wv, np.float32),
        Wo=np.ascontiguousarray(L0_attn_Wo, np.float32),
        Wg=np.ascontiguousarray(L0_attn_Wg, np.float32),
        Wog=np.ascontiguousarray(L0_attn_Wog, np.float32),
        Wd=np.ascontiguousarray(L0_attn_Wd, np.float32),
        bd=np.ascontiguousarray(np.asarray(L0_attn_bd, np.float32).reshape(16, 1)),
        nw=np.ascontiguousarray(np.asarray(L0_attn_norm_w, np.float32).reshape(1, -1)),
        w_gate=np.ascontiguousarray(L0_ffn_w_gate, np.float32),
        w_up=np.ascontiguousarray(L0_ffn_w_up, np.float32),
        w_down=np.ascontiguousarray(L0_ffn_w_down, np.float32),
        **consts,
    )
    in_maps = []
    for c in range(n_cores):
        xc = np.ascontiguousarray(x[c * per:(c + 1) * per].reshape(per * S, -1))
        in_maps.append(dict(x=xc, **common))
    res = run_bass_kernel_spmd(nc, in_maps, core_ids=list(range(n_cores)))
    outs = [np.asarray(res.results[c]["out"]).reshape(per, S, -1)
            for c in range(n_cores)]
    return np.concatenate(outs, axis=0)


# version 29 (final)
# speedup vs baseline: 1.0160x; 1.0160x over previous
import contextlib

import numpy as np
import ml_dtypes
import concourse.bass as bass
import concourse.mybir as mybir
from concourse.bass_utils import run_bass_kernel_spmd

F32 = mybir.dt.float32
BF16 = mybir.dt.bfloat16
I16 = mybir.dt.int16
U16 = mybir.dt.uint16
AF = mybir.ActivationFunctionType
ALU = mybir.AluOpType
AX = mybir.AxisListType
BF = ml_dtypes.bfloat16

ENGINES = ("tensor", "vector", "scalar", "gpsimd", "sync")

DM = 1024
H = 16
DH = 64
CH = 128
NI = 3072
NIC = NI // 128
QMAX = 32767.0
LN_EPS = 1e-5
RMS_EPS = float(np.finfo(np.float32).eps)


class Buf:
    __slots__ = ("name", "writer", "readers")

    def __init__(self, name):
        self.name = name
        self.writer = None
        self.readers = []


class Prog:

    def __init__(self, nc):
        self.nc = nc
        self.stack = contextlib.ExitStack()
        self.items = {e: [] for e in ENGINES}
        self.ccount = {e: 0 for e in ENGINES}
        self.dcount = {}
        self.csem = {}
        self.dsem = {}
        self.watermark = {}

    def sbuf(self, name, shape, dtype):
        return self.stack.enter_context(self.nc.sbuf_tensor(name, shape, dtype))

    def psum(self, name, shape, dtype=F32):
        return self.stack.enter_context(self.nc.psum_tensor(name, shape, dtype))

    def _sem(self, key):
        kind, e = key
        d = self.csem if kind == "c" else self.dsem
        if e not in d:
            d[e] = self.stack.enter_context(self.nc.semaphore(f"{kind}_{e}"))
        return d[e]

    def _collect(self, engine, reads, writes):
        stamps = set()
        for b in reads:
            if b.writer is not None:
                stamps.add(b.writer)
        for b in writes:
            if b.writer is not None:
                stamps.add(b.writer)
            for r in b.readers:
                stamps.add(r)
        waits = []
        for kind, src, seq in stamps:
            key = (kind, src)
            val = seq + 1 if kind == "c" else 16 * self.dcount[src]
            wm = self.watermark.get((engine, key), 0)
            if wm >= val:
                continue
            self.watermark[(engine, key)] = val
            waits.append((key, val))
        return waits

    def _record(self, stamp, reads, writes):
        for b in reads:
            b.readers.append(stamp)
        for b in writes:
            b.writer = stamp
            b.readers = []

    def op(self, engine, emit, reads=(), writes=(), tag=""):
        waits = self._collect(engine, reads, writes)
        seq = self.ccount[engine]
        self.ccount[engine] += 1
        self.items[engine].append((waits, emit, ("c", engine), 1, tag))
        self._record(("c", engine, seq), reads, writes)

    def dma(self, queue, out, in_, reads=(), writes=(), tag="", sem_group=None,
            **kw):
        assert writes or sem_group, tag
        grp = f"{queue}_{sem_group or writes[0].name}"
        waits = self._collect(queue, reads, writes)
        seq = self.dcount.get(grp, 0)
        self.dcount[grp] = seq + 1

        def emit(eng, out=out, in_=in_, kw=kw):
            return eng.dma_start(out=out, in_=in_, **kw)

        self.items[queue].append((waits, emit, ("d", grp), 16, tag))
        self._record(("d", grp, seq), reads, writes)

    def build(self):
        nc = self.nc
        with nc.Block() as block:
            for ename in ENGINES:
                if not self.items[ename]:
                    continue

                def make_body(ename=ename):
                    def body(eng):
                        for waits, emit, inc_key, inc_n, tag in self.items[ename]:
                            for key, val in waits:
                                eng.wait_ge(self._sem(key), val)
                            emit(eng).then_inc(self._sem(inc_key), inc_n)
                        if ename == "sync":
                            for grp, cnt in self.dcount.items():
                                eng.wait_ge(self._sem(("d", grp)), 16 * cnt)

                    return body

                getattr(block, ename)(make_body())
        return nc


def build_program(n_seq=2, S=2048, n_cores=8, stage=99):
    NT = n_seq * S
    NC_ = S // CH
    TT1 = max(1, NT // 512)
    nc = bass.Bass("TRN2", target_bir_lowering=False)
    p = Prog(nc)

    def din(name, shape, dt):
        return nc.dram_tensor(name, shape, dt, kind="ExternalInput").ap()

    x_in = din("x", [NT, DM], F32)
    Wq = din("Wq", [DM, DM], F32)
    Wk = din("Wk", [DM, DM], F32)
    Wv = din("Wv", [DM, DM], F32)
    Wo = din("Wo", [DM, DM], F32)
    Wg = din("Wg", [DM, DM], F32)
    Wog = din("Wog", [DM, DM], F32)
    Wd = din("Wd", [H, DM], F32)
    bd = din("bd", [H, 1], F32)
    nw = din("nw", [1, DM], F32)
    w_gate = din("w_gate", [NI, DM], F32)
    w_up = din("w_up", [NI, DM], F32)
    w_down = din("w_down", [DM, NI], F32)
    idbf_d = din("idbf", [128, 128], BF16)
    idf_d = din("idf32", [128, 128], F32)
    idt_d = din("idt", [128, 1024], BF16)
    sel_d = din("sel", [2, 16, 1024], BF16)
    ones16_d = din("ones16", [16, 128], F32)
    aconst_d = din("aconst", [1, 8], F32)
    pm_d = din("pmask", [16, 8], F32)
    eo_d = din("eomask", [16, 128], F32)

    out_d = nc.dram_tensor("out", [NT, DM], F32, kind="ExternalOutput").ap()

    x1_s = nc.dram_tensor("x1_s", [NT, DM], F32).ap()
    ln2t_s = nc.dram_tensor("ln2t_s", [8, 128, NT], BF16).ap()
    upt_s = nc.dram_tensor("upt_s", [NIC, 128, NT], BF16).ap()
    gat_s = nc.dram_tensor("gat_s", [NIC, 128, NT], BF16).ap()
    tra_s = nc.dram_tensor("tra_s", [NIC, 128, NT], BF16).ap()
    ora_s = nc.dram_tensor("ora_s", [NIC, 128, NT], BF16).ap()
    wut_s = nc.dram_tensor("wut_s", [NIC, 128, 1024], BF16).ap()
    wgt_s = nc.dram_tensor("wgt_s", [NIC, 128, 1024], BF16).ap()
    wdnt_s = nc.dram_tensor("wdnt_s", [NIC, 128, 1024], BF16).ap()
    cc_in = [nc.dram_tensor(f"cc{i}_in", [1, 2], F32) for i in range(4)]
    cc_out = [nc.dram_tensor(f"cc{i}_out", [1, 2], F32) for i in range(4)]

    AB_COLS = 72064
    AB = p.sbuf("AB", [128, AB_COLS], BF16)
    ab_off = [0]

    def abreg(cols, name):
        o = ab_off[0]
        ab_off[0] += cols
        assert ab_off[0] <= AB_COLS, (name, ab_off[0])
        return o

    def abv(off, cols, dt=BF16):
        ap = AB[:, off:off + cols]
        if dt is not BF16:
            ap = ap.bitcast(dt)
        return ap

    o_wt = {w: abreg(8192, w) for w in ("q", "k", "v", "g", "og")}
    o_wot = abreg(8192, "wot")
    o_wdt = abreg(128, "wdt")
    o_stage = abreg(8192, "stage")
    o_ln = abreg(1024, "ln")
    o_lnT = abreg(1024, "lnT")
    o_q = abreg(1024, "q")
    o_thxg = abreg(1024, "thxg")
    o_kt = abreg(1024, "kt")
    o_v = abreg(1024, "v")
    o_thog = abreg(1024, "thog")
    o_kT = abreg(1024, "kT")
    o_qT = abreg(1024, "qT")
    o_Dt = abreg(1024, "Dt")
    o_At = abreg(1024, "At")
    o_k2 = abreg(1024, "k2")
    o_lnT2 = abreg(1024, "lnT2")
    o_s0bf = abreg(1024, "s0bf")
    o_rhl = abreg(256, "rhl")
    OB1S = o_wt["og"]
    OB2S = 0
    OB_WDN = o_wt["og"] + 8192
    OB_XO = 2 * 4096

    AF_COLS = 12288
    AFA = p.sbuf("AFA", [128, AF_COLS], F32)
    af_off = [0]

    def afreg(cols, name):
        o = af_off[0]
        af_off[0] += cols
        assert af_off[0] <= AF_COLS, (name, af_off[0])
        return o

    def afv(off, cols):
        return AFA[:, off:off + cols]

    o_x = [afreg(1024, f"x{i}") for i in range(2)]
    o_attn = afreg(1024, "attn")
    o_tmpO = afreg(1024, "tmpO")
    o_x1 = afreg(1024, "x1")
    o_S0 = afreg(1024, "S0")
    o_sm = afreg(96, "sm")
    o_g = afreg(512, "gtiles")
    o_am = afreg(80, "am")
    o_nw = afreg(16, "nw")
    o_sc = afreg(64, "scales")
    o_wsc = afreg(192, "wsc")
    o_wraw = afreg(2048, "wraw")
    o_dq = afreg(3072, "dq")

    t_bd = p.sbuf("t_bd", [16, 1], F32)
    t_c16 = p.sbuf("t_c16", [16, 160], F32)
    t_id = p.sbuf("t_id", [128, 128], BF16)
    t_idf = p.sbuf("t_idf", [128, 128], F32)
    t_idt = p.sbuf("t_idt", [128, 1024], BF16)
    t_sel = p.sbuf("t_sel", [16, 2048], BF16)
    t_ones16 = p.sbuf("t_ones", [16, 128], F32)
    t_pm = p.sbuf("t_pm", [16, 8], F32)
    t_eo = p.sbuf("t_eo", [16, 128], F32)
    t_wd16 = p.sbuf("t_wd16", [16, 1024], BF16)
    t_nwrow = p.sbuf("t_nwrow", [1, 1024], F32)
    t_g64 = p.sbuf("t_g64", [64, 128], F32)
    t_ac = p.sbuf("t_ac", [128, 8], F32)

    PP = [p.psum(f"PP{i}", [128, 1024], F32) for i in range(4)]

    B = {}

    def gb(name):
        if name not in B:
            B[name] = Buf(name)
        return B[name]

    pb = [gb(f"pb{i}") for i in range(8)]

    def pp(i, half=None):
        if half is None:
            return PP[i][:, :], [pb[2 * i], pb[2 * i + 1]]
        return PP[i][:, 512 * half:512 * (half + 1)], [pb[2 * i + half]]

    ACONSTS = [LN_EPS, RMS_EPS, 1.0, -1.0, 0.5, -0.5, 0.0, 1.0 / 65536.0]
    p.dma("gpsimd", out=t_ac[:, :],
          in_=aconst_d.partition_broadcast(128).squeeze(1), writes=[gb("actc")],
          sem_group="init")
    for ci, cv in enumerate(ACONSTS):
        nc.const_aps.register(F32, cv, t_ac[:, ci:ci + 1])
    for _e in ("tensor", "vector", "scalar", "gpsimd"):
        p.op(_e, lambda eng: eng.nop(), reads=[gb("actc")], tag=f"acwait_{_e}")

    for ap_, t_, bn in ((idbf_d, t_id, "id"), (idf_d, t_idf, "idf"),
                        (idt_d, t_idt, "idt"), (ones16_d, t_ones16, "ones16"),
                        (pm_d, t_pm, "pm"), (eo_d, t_eo, "eo"),
                        (bd, t_bd, "bd"), (nw, t_nwrow, "nwrow")):
        p.dma("sync", out=t_[tuple(slice(None) for _ in t_.shape)], in_=ap_,
              writes=[gb(bn)], sem_group="init")
    p.dma("sync", out=t_sel[:, :].rearrange("a (g c) -> a g c", g=2),
          in_=sel_d.rearrange("g k c -> k g c"), writes=[gb("sel")],
          sem_group="init")

    nwcol = afv(o_nw, 8)

    def emit_nwT(t):
        last = None
        for k in range(8):
            last = t.transpose(PP[3][:, 64 + k:64 + k + 1],
                               t_nwrow[:, 128 * k:128 * (k + 1)], t_idf[0:1, 0:1])
        return last
    p.op("tensor", emit_nwT, reads=[gb("nwrow"), gb("idf")], writes=[pb[6]])
    p.op("vector", lambda v: v.tensor_scalar(
        out=nwcol[:, :], in0=PP[3][:, 64:72], scalar1=0.5, scalar2=None,
        op0=ALU.mult), reads=[pb[6]], writes=[gb("nw")])

    stage_sl = [abv(o_stage, 8192),
                AFA[:, o_wraw:o_wraw + 4096].bitcast(BF16)]
    wq_rr = ["gpsimd", "gpsimd", "gpsimd"]
    wp_ct = [0]

    def prep_weight(wname, src, out_off, scale_col=None):
        wi = wp_ct[0]
        wp_ct[0] += 1
        stage = stage_sl[wi % 2]
        bs = gb("stage0") if wi % 2 == 0 else gb("stageB")
        p.dma(wq_rr[wi % 3],
              out=stage.rearrange("p (i d) -> p i d", i=8),
              in_=src.rearrange("(i p) d -> p i d", p=128),
              writes=[bs], tag=f"w{wname}ld")
        wt_b = gb("wt_" + wname)
        for k in range(8):
            def tp(t, stage=stage, k=k):
                last = None
                for i in range(8):
                    last = t.transpose(
                        PP[3][:, 0:512].bitcast(BF16)[:, 128 * i:128 * (i + 1)],
                        stage[:, 1024 * i + 128 * k: 1024 * i + 128 * (k + 1)],
                        t_id[:, :])
                return last
            p.op("tensor", tp, reads=[bs, gb("id")], writes=[pb[6]],
                 tag=f"w{wname}tp{k}")
            dst = abv(out_off, 8192).rearrange("p (k o) -> p k o", k=8)[:, k, :]
            if scale_col is None:
                p.op("scalar", lambda s, dst=dst: s.copy(
                    out=dst, in_=PP[3][:, 0:512].bitcast(BF16)),
                    reads=[pb[6]], writes=[wt_b], tag=f"w{wname}ev{k}")
            else:
                p.op("scalar", lambda s, dst=dst, k=k: s.activation(
                    out=dst, in_=PP[3][:, 0:512].bitcast(BF16), func=AF.Identity,
                    scale=nwcol[:, k:k + 1], bias=0.0),
                    reads=[pb[6], gb("nw")], writes=[wt_b], tag=f"w{wname}ev{k}")
        return wt_b

    bw = {}
    bw["q"] = prep_weight("q", Wq, o_wt["q"])
    bw["k"] = prep_weight("k", Wk, o_wt["k"])
    bw["g"] = prep_weight("g", Wg, o_wt["g"])

    p.dma("gpsimd", out=t_wd16[:, :], in_=Wd, writes=[gb("wd16")],
          sem_group="init")

    def wd_tp(t):
        last = None
        for k in range(8):
            last = t.transpose(
                PP[3][:, 0:512].bitcast(BF16)[:, 16 * k:16 * (k + 1)],
                t_wd16[:, 128 * k:128 * (k + 1)], t_id[0:16, 0:16])
        return last
    p.op("tensor", wd_tp, reads=[gb("wd16"), gb("id")], writes=[pb[6]])
    p.op("scalar", lambda s: s.copy(out=abv(o_wdt, 128),
         in_=PP[3][:, 0:64].bitcast(BF16)), reads=[pb[6]], writes=[gb("wdt")])

    am64 = afv(o_am, 64)
    S0 = afv(o_S0, 1024)
    S0bf = abv(o_s0bf, 1024)
    p.op("vector", lambda v: v.memset(S0[:, :], 0.0), writes=[gb("S0")])
    p.op("vector", lambda v: v.memset(S0bf[:, :], 0.0), writes=[gb("S0bf")])
    p.op("vector", lambda v: v.memset(am64[:, :], 0.0), writes=[gb("am64")])

    wraw = [afv(o_wraw, 1024), afv(o_wraw + 1024, 1024)]
    bwraw = [gb("wraw0"), gb("wraw1")]
    wsc = afv(o_wsc, 96)
    wrec = afv(o_wsc + 96, 96)
    dq0_sl = [afv(o_dq, 1024), afv(o_dq + 1024, 1024)]
    dq2 = afv(o_dq + 2048, 1024)
    wqi_sl = [abv(o_stage, 1024, I16), abv(o_stage + 7168, 1024, I16)]
    wdq_st = [abv(o_stage + 1024 + 1024 * j, 1024) for j in range(2)] + \
        [abv(o_stage + 6144, 1024)]
    bwdq = [gb("wdq0"), gb("wdq1"), gb("wdq2")]
    wev_st = [abv(o_stage + 3072 + 1024 * j, 1024) for j in range(2)]
    bwev = [gb("wev0"), gb("wev1")]
    def emit_stfence():
        p.op("vector", lambda v: v.memset(AFA[0:1, o_sc:o_sc + 1], 0.0),
             reads=[],
             writes=[gb("stage0"), gb("stageB"),
                     gb("wqi0"), gb("wqi1"), bwdq[0], bwdq[1], bwdq[2],
                     bwev[0], bwev[1], bwraw[0], bwraw[1],
                     gb("dq00"), gb("dq01")], tag="stfence")

    def front_dma(uidx):
        wi = uidx % 2
        p.dma("gpsimd", out=wraw[wi][:, :], in_=units[uidx][0],
              writes=[bwraw[wi]], tag=f"wu_ld{uidx}")

    def front_calc_a(uidx):
        wi = uidx % 2
        braw = bwraw[wi]
        sl = 32 * (uidx % 3)
        vsc = wraw[wi][:, :].rearrange("p (b c) -> p b c", c=32)
        wsc_u = wsc[:, sl:sl + 32]
        wrec_u = wrec[:, sl:sl + 32]
        recb = wrec_u.unsqueeze(2).broadcast_to([128, 32, 32])
        wqi_u = wqi_sl[uidx % 2]
        bwqi_u = gb(f"wqi{uidx % 2}")
        bws = gb(f"wsc{uidx % 3}")
        bwr = gb(f"wrec{uidx % 3}")
        p.op("vector", lambda v: v.tensor_reduce(out=wsc_u, in_=vsc,
             axis=AX.X, op=ALU.max, apply_absolute_value=True),
             reads=[braw], writes=[bws], tag=f"wu_mx{uidx}")
        p.op("vector", lambda v: v.tensor_scalar(out=wsc_u,
             in0=wsc_u, scalar1=1.0 / 7.0, scalar2=1e-12, op0=ALU.mult,
             op1=ALU.max), reads=[bws], writes=[bws])
        p.op("vector", lambda v: v.reciprocal(out=wrec_u,
             in_=wsc_u), reads=[bws], writes=[bwr])
        p.op("vector", lambda v, wqi_u=wqi_u: v.tensor_tensor(
             out=wqi_u.rearrange("p (b c) -> p b c", c=32), in0=vsc, in1=recb,
             op=ALU.mult), reads=[braw, bwr], writes=[bwqi_u])
        dq0_u = dq0_sl[uidx % 2]
        bdq0_u = gb(f"dq0{uidx % 2}")
        p.op("gpsimd", lambda g, dq0_u=dq0_u: g.tensor_tensor(
             out=dq0_u[:, :].rearrange("p (b c) -> p b c", c=32), in0=vsc,
             in1=recb, op=ALU.mult), reads=[braw, bwr],
             writes=[bdq0_u])
        p.op("gpsimd", lambda g, wqi_u=wqi_u, dq0_u=dq0_u: g.tensor_tensor(
             out=dq0_u[:, :], in0=dq0_u[:, :], in1=wqi_u, op=ALU.subtract),
             reads=[bdq0_u, bwqi_u], writes=[bdq0_u])

    def front_calc_b(uidx):
        sl = 32 * (uidx % 3)
        scb = wsc[:, sl:sl + 32].unsqueeze(2).broadcast_to([128, 32, 32])
        wdq_u = wdq_st[uidx % 3]
        bwdq_u = bwdq[uidx % 3]
        wqi_u = wqi_sl[uidx % 2]
        bwqi_u = gb(f"wqi{uidx % 2}")
        dq0_u = dq0_sl[uidx % 2]
        bdq0_u = gb(f"dq0{uidx % 2}")
        p.op("vector", lambda v, dq0_u=dq0_u: v.tensor_scalar(out=dq2[:, :],
             in0=dq0_u[:, :], scalar1=0.5, scalar2=None, op0=ALU.is_ge),
             reads=[bdq0_u], writes=[gb("dq2")])
        p.op("vector", lambda v, dq0_u=dq0_u: v.scalar_tensor_tensor(
             out=dq2[:, :], in0=dq0_u[:, :], scalar=-0.5, in1=dq2[:, :],
             op0=ALU.is_le, op1=ALU.subtract),
             reads=[bdq0_u, gb("dq2")], writes=[gb("dq2")])
        p.op("gpsimd", lambda g, wqi_u=wqi_u: g.tensor_tensor(out=dq2[:, :],
             in0=wqi_u, in1=dq2[:, :], op=ALU.subtract),
             reads=[gb("dq2"), bwqi_u], writes=[gb("dq2")])
        p.op("gpsimd", lambda g, wdq_u=wdq_u: g.tensor_tensor(
             out=wdq_u.rearrange("p (b c) -> p b c", c=32),
             in0=dq2[:, :].rearrange("p (b c) -> p b c", c=32), in1=scb,
             op=ALU.mult), reads=[gb("dq2"), gb(f"wsc{uidx % 3}")],
             writes=[bwdq_u])

    def unit_back(uidx, dst_ap, bdst):
        wdq_u = wdq_st[uidx % 3]
        bwdq_u = bwdq[uidx % 3]
        hh = uidx % 2
        ph = PP[1][:, 512 * hh:512 * (hh + 1)]
        pbh = pb[2 + hh]

        def tp(t, wdq_u=wdq_u, ph=ph):
            last = None
            for k in range(8):
                last = t.transpose(
                    ph.bitcast(BF16)[:, 128 * k:128 * (k + 1)],
                    wdq_u[:, 128 * k:128 * (k + 1)], t_id[:, :])
            return last
        p.op("tensor", tp, reads=[bwdq_u, gb("id")], writes=[pbh],
             tag=f"wu_tp{uidx}")
        ev = wev_st[uidx % 2]
        p.op("scalar", lambda s, ev=ev, ph=ph: s.copy(out=ev[:, :],
             in_=ph.bitcast(BF16)), reads=[pbh],
             writes=[bwev[uidx % 2]], tag=f"wu_ev{uidx}")
        p.dma("scalar", out=dst_ap, in_=ev[:, :], reads=[bwev[uidx % 2]],
              writes=[bdst], tag=f"wu_st{uidx}")

    units = []
    for i in range(NIC):
        units.append((w_up[128 * i:128 * (i + 1), :], wut_s[i], gb("wuts")))
        units.append((w_gate[128 * i:128 * (i + 1), :], wgt_s[i], gb("wgts")))
    for r in range(8):
        for ii in range(3):
            units.append((w_down[128 * r:128 * (r + 1),
                                 1024 * ii:1024 * (ii + 1)], None, (r, ii)))
    dma_ct = [0]
    aca_ct = [0]
    acb_ct = [0]
    back_ct = [0]

    NU = len(units)
    NTOT = n_seq * NC_

    def unit_dst(u):
        src, dst, bdst = units[u]
        if dst is not None:
            return dst, bdst
        r, ii = bdst
        dst_ap = wdnt_s.rearrange("i p d -> p i d")[
            :, 8 * ii:8 * (ii + 1), 128 * r:128 * (r + 1)]
        return dst_ap, gb("wdns")

    def emit_fdma():
        u = dma_ct[0]
        if u >= len(units) or u > aca_ct[0] + 1:
            return
        dma_ct[0] += 1
        front_dma(u)

    def emit_fa():
        u = aca_ct[0]
        if u >= dma_ct[0] or u > acb_ct[0] + 1:
            return
        aca_ct[0] += 1
        front_calc_a(u)

    def emit_fb():
        u = acb_ct[0]
        if u >= aca_ct[0] or u > back_ct[0] + 2:
            return
        acb_ct[0] += 1
        front_calc_b(u)

    def emit_back():
        u = back_ct[0]
        if u >= acb_ct[0] or u >= len(units):
            return
        back_ct[0] += 1
        dst_ap, bdst = unit_dst(u)
        unit_back(u, dst_ap, bdst)

    WT = {w: abv(o_wt[w], 8192).rearrange("p (k o) -> p k o", k=8)
          for w in ("q", "k", "v", "g", "og")}
    WOT = abv(o_wot, 8192).rearrange("p (k o) -> p k o", k=8)
    WDT = abv(o_wdt, 128).rearrange("p (k h) -> p k h", k=8)

    x_t = [afv(o, 1024) for o in o_x]
    bx = [gb(f"bx{i}") for i in range(2)]
    ln = abv(o_ln, 1024)
    lnT_sl = [abv(o_lnT, 1024), abv(o_stage + 5120, 1024)]
    q_tok = abv(o_q, 1024)
    thxg = abv(o_thxg, 1024)
    kt = abv(o_kt, 1024)
    v_h = abv(o_v, 1024)
    thog = abv(o_thog, 1024)
    kT = abv(o_kT, 1024)
    qT = abv(o_qT, 1024)
    Dt = abv(o_Dt, 1024)
    At = abv(o_At, 1024)
    k2 = abv(o_k2, 1024)
    z_bf = abv(o_q, 1024)
    y_bf = abv(o_thxg, 1024)
    yT = abv(o_kt, 1024)
    ln2 = abv(o_v, 1024)
    lnT2 = abv(o_lnT2, 1024)
    attn = afv(o_attn, 1024)
    tmpO = afv(o_tmpO, 1024)
    x1_t = afv(o_x1, 1024)
    rhi = abv(o_rhl, 128)
    rlo = abv(o_rhl + 128, 128)

    SMS = [afv(o_sm + 32 * j, 32) for j in range(3)]

    c16 = t_c16
    sp16 = AFA[0:16, o_g:o_g + 128]
    expg0 = AFA[0:16, o_g + 128:o_g + 256]
    GT = t_g64[0:16, :]
    GmT = t_g64[32:48, :]
    expG = AFA[:, o_g + 256:o_g + 272]
    E2 = AFA[:, o_g + 272:o_g + 288]
    expGCp = AFA[:, o_g + 288:o_g + 296]

    def ln1_stats(s, c, idx):
        xi = x_t[idx % 2]
        bxi = bx[idx % 2]
        sm = SMS[0]
        bns = sm[:, 6:18]
        p.op("vector", lambda v: v.bn_stats(out=bns[:, 0:6], in_=xi[:, 0:512]),
             reads=[bxi], writes=[gb("bns0")])
        p.op("vector", lambda v: v.bn_stats(out=bns[:, 6:12], in_=xi[:, 512:1024]),
             reads=[bxi], writes=[gb("bns0")])
        p.op("vector", lambda v: v.bn_aggr(out=sm[:, 4:6], in_=bns[:, :]),
             reads=[gb("bns0")], writes=[gb("mv0")])
        p.op("scalar", lambda s_: s_.activation(out=sm[:, 2:3], in_=sm[:, 5:6],
             func=AF.Ln, bias=LN_EPS, scale=1.0), reads=[gb("mv0")],
             writes=[gb("lv0")])
        p.op("scalar", lambda s_: s_.activation(out=sm[:, 0:1], in_=sm[:, 2:3],
             func=AF.Exp, bias=0.0, scale=-0.5), reads=[gb("lv0")],
             writes=[gb("rstd0")])

    def ln1_apply(s, c, idx):
        xi = x_t[idx % 2]
        bxi = bx[idx % 2]
        sm = SMS[0]
        p.op("vector", lambda v: v.tensor_scalar(out=sm[:, 1:2], in0=sm[:, 4:5],
             scalar1=sm[:, 0:1], scalar2=-1.0, op0=ALU.mult, op1=ALU.mult),
             reads=[gb("mv0"), gb("rstd0")], writes=[gb("negmr0")])
        p.op("vector", lambda v: v.tensor_scalar(out=ln[:, :], in0=xi[:, :],
             scalar1=sm[:, 0:1], scalar2=sm[:, 1:2], op0=ALU.mult,
             op1=ALU.add),
             reads=[bxi, gb("rstd0"), gb("negmr0"), gb("ln")], writes=[gb("ln")])

    ldps = PP[3][0:16, 512:640]

    def emit_lnT(idx):
        lnT = lnT_sl[idx % 2]
        blnT = gb(f"lnT{idx % 2}")

        def lnT_tp(t):
            last = None
            for k in range(8):
                last = t.transpose(
                    PP[3][:, 0:512].bitcast(BF16)[:, 128 * k:128 * (k + 1)],
                    ln[:, 128 * k:128 * (k + 1)], t_id[:, :])
            return last
        p.op("tensor", lnT_tp, reads=[gb("ln"), gb("id")], writes=[pb[6]],
             tag=f"lnTtp{idx}")
        p.op("scalar", lambda s_, lnT=lnT: s_.copy(out=lnT[:, :],
             in_=PP[3][:, 0:512].bitcast(BF16)), reads=[pb[6]], writes=[blnT],
             tag=f"lnTev{idx}")

    def emit_ld(idx):
        lnTv = lnT_sl[idx % 2].rearrange("p (k t) -> p k t", k=8)
        blnT = gb(f"lnT{idx % 2}")

        def ldmm(t, lnTv=lnTv):
            last = None
            for k in range(8):
                last = t.matmul(ldps, WDT[:, k, :], lnTv[:, k, :],
                                start=(k == 0), stop=(k == 7))
            return last
        p.op("tensor", ldmm, reads=[blnT, gb("wdt")], writes=[pb[7]],
             tag=f"ldmm{idx}")

    def emit_qkg(idx):
        lnTv = lnT_sl[idx % 2].rearrange("p (k t) -> p k t", k=8)
        blnT = gb(f"lnT{idx % 2}")

        def qg(t, lnTv=lnTv):
            last = None
            for k in range(8):
                st, sp_ = (k == 0), (k == 7)
                for (wn, ppi) in (("q", 0), ("g", 2)):
                    t.matmul(PP[ppi][:, 0:512], lnTv[:, k, :],
                             WT[wn][:, k, 0:512], start=st, stop=sp_)
                    last = t.matmul(PP[ppi][:, 512:1024], lnTv[:, k, :],
                                    WT[wn][:, k, 512:1024], start=st, stop=sp_)
            return last
        p.op("tensor", qg, reads=[blnT, bw["q"], bw["g"]],
             writes=[pb[0], pb[1], pb[4], pb[5]], tag=f"qg{idx}")

        def kk(t, lnTv=lnTv):
            last = None
            for k in range(8):
                st, sp_ = (k == 0), (k == 7)
                t.matmul(PP[1][:, 0:512], lnTv[:, k, :],
                         WT["k"][:, k, 0:512], start=st, stop=sp_)
                last = t.matmul(PP[1][:, 512:1024], lnTv[:, k, :],
                                WT["k"][:, k, 512:1024], start=st, stop=sp_)
            return last
        p.op("tensor", kk, reads=[blnT, bw["k"]],
             writes=[pb[2], pb[3]], tag=f"kk{idx}")

    a16 = c16[:, 0:128]
    expGC16 = c16[:, 144:145]
    prhs = c16[:, 152:160]

    def ldg_front(idx):
        p.op("scalar", lambda s_: s_.activation(out=a16[:, :], in_=ldps,
             func=AF.Abs, bias=t_bd[:, :], scale=1.0),
             reads=[pb[7], gb("bd")], writes=[gb("a16")])
        p.op("scalar", lambda s_: s_.activation(out=sp16[:, :], in_=ldps,
             func=AF.Relu, bias=t_bd[:, :], scale=1.0),
             reads=[pb[7], gb("bd")], writes=[gb("sp")])
        p.op("scalar", lambda s_: s_.activation(out=a16[:, :], in_=a16[:, :],
             func=AF.Exp, bias=0.0, scale=-1.0), reads=[gb("a16")],
             writes=[gb("a16")])
        p.op("scalar", lambda s_: s_.activation(out=a16[:, :], in_=a16[:, :],
             func=AF.Ln, bias=1.0, scale=1.0), reads=[gb("a16")],
             writes=[gb("a16")])
        p.op("vector", lambda v: v.tensor_tensor(out=sp16[:, :], in0=sp16[:, :],
             in1=a16[:, :], op=ALU.add), reads=[gb("a16"), gb("sp")],
             writes=[gb("sp")])
        p.op("vector", lambda v: v.tensor_tensor_scan(out=GT[:, :],
             data0=t_ones16[:, :], data1=sp16[:, :], initial=0.0,
             op0=ALU.mult, op1=ALU.subtract),
             reads=[gb("sp"), gb("ones16")], writes=[gb("GT")],
             tag=f"gscan{idx}")
        p.op("scalar", lambda s_: s_.activation(out=expg0[:, :], in_=sp16[:, :],
             func=AF.Exp, bias=0.0, scale=-1.0), reads=[gb("sp")],
             writes=[gb("expg")])
        p.op("vector", lambda v: v.memset(expg0[:, 0:1], 0.0),
             writes=[gb("expg")])
        p.op("scalar", lambda s_: s_.copy(out=rhi[0:16, :], in_=expg0[:, :]),
             reads=[gb("expg")], writes=[gb("rhi")])
        p.op("vector", lambda v: v.tensor_tensor(out=rlo[0:16, :],
             in0=expg0[:, :], in1=rhi[0:16, :], op=ALU.subtract),
             reads=[gb("expg"), gb("rhi")], writes=[gb("rlo")])
        p.op("vector", lambda v: v.tensor_scalar(out=GmT[:, :], in0=GT[:, :],
             scalar1=GT[:, 127:128], scalar2=None, op0=ALU.subtract),
             reads=[gb("GT")], writes=[gb("GmT")])
        p.op("scalar", lambda s_: s_.activation(out=expGC16[:, :],
             in_=GT[:, 127:128], func=AF.Exp, bias=0.0, scale=1.0),
             reads=[gb("GT")], writes=[gb("gc16")])
        p.op("vector", lambda v: v.tensor_scalar(out=prhs[:, :], in0=t_pm[:, :],
             scalar1=expGC16[:, :], scalar2=None, op0=ALU.mult),
             reads=[gb("gc16"), gb("pm")], writes=[gb("prhs")])

    def ldg_back(idx):
        p.op("tensor", lambda t: t.transpose(PP[3][:, 512:576], t_g64[:, :],
             t_idf[0:64, 0:64]), reads=[gb("GT"), gb("GmT"), gb("idf")],
             writes=[pb[7]], tag=f"ge2tp{idx}")
        p.op("scalar", lambda s_: s_.activation(out=expG[:, :],
             in_=PP[3][:, 512:528], func=AF.Exp, bias=0.0, scale=1.0),
             reads=[pb[7]], writes=[gb("expG")])
        p.op("scalar", lambda s_: s_.activation(out=E2[:, :],
             in_=PP[3][:, 544:560], func=AF.Exp, bias=0.0, scale=-1.0),
             reads=[pb[7]], writes=[gb("E2")])
        p.op("tensor", lambda t: t.matmul(PP[3][:, 576:584], t_eo[:, :],
             prhs[:, :], start=True, stop=True), reads=[gb("prhs"), gb("eo")],
             writes=[pb[7]], tag=f"gcmm{idx}")

    def expgcp_emit(idx):
        p.op("vector", lambda v: v.tensor_copy(out=expGCp[:, :],
             in_=PP[3][:, 576:584]), reads=[], writes=[pb[7], gb("expGCp")],
             tag=f"egcp{idx}")

    def rmm_g(g, idx):
        ppR = PP[2 + g]
        pbR = [pb[4 + 2 * g], pb[5 + 2 * g]]

        def rmm(t, g=g, ppR=ppR):
            last = None
            for j in range(8):
                sel = t_sel[:, 1024 * g + 128 * j: 1024 * g + 128 * (j + 1)]
                t.matmul(ppR[:, 128 * j:128 * (j + 1)], sel, rhi[0:16, :],
                         start=True, stop=False)
                last = t.matmul(ppR[:, 128 * j:128 * (j + 1)], sel,
                                rlo[0:16, :], start=False, stop=True)
            return last
        p.op("tensor", rmm, reads=[gb("rhi"), gb("rlo"), gb("sel")],
             writes=pbR, tag=f"rmm{g}_{idx}")

    def scan_g(g, idx):
        ppR = PP[2 + g]
        pbR = [pb[4 + 2 * g], pb[5 + 2 * g]]
        dt_g = Dt if g == 0 else At
        bdt = gb("Dt") if g == 0 else gb("At")
        p.op("vector", lambda v, ppR=ppR, dt_g=dt_g: v.tensor_tensor_scan(
             out=dt_g[:, :], data0=ppR[:, :], data1=t_idt[:, :], initial=0.0,
             op0=ALU.mult, op1=ALU.add),
             reads=pbR + [gb("idt")], writes=[bdt], tag=f"scan{g}_{idx}")

    def fslot(want, n=1):
        for _ in range(n):
            if aca_ct[0] < want:
                emit_fb()
                emit_fa()
                emit_fdma()

    def chunk(s, c, idx, handoff=None, pre=None, post=None):
        xi = x_t[idx % 2]
        bxi = bx[idx % 2]
        row0 = s * S + c * CH
        sfx = f"_{s}_{c}"

        emit_fdma()
        want_u = (idx + 1) * NU // NTOT + 1

        p.op("scalar", lambda s_: s_.activation(out=thxg[:, :], in_=PP[2][:, :],
             func=AF.Tanh, bias=0.0, scale=0.5),
             reads=[pb[4], pb[5], gb("thxg")], writes=[gb("thxg")])
        p.op("scalar", lambda s_: s_.copy(out=q_tok[:, :], in_=PP[0][:, :]),
             reads=[pb[0], pb[1], gb("q")], writes=[gb("q")])
        p.op("vector", lambda v: v.scalar_tensor_tensor(out=kt[:, :],
             in0=thxg[:, :], scalar=1.0, in1=PP[1][:, :], op0=ALU.add,
             op1=ALU.mult), reads=[gb("thxg"), pb[2], pb[3], gb("kt")],
             writes=[gb("kt")])

        def q_tp(t):
            last = None
            for j in range(8):
                last = t.transpose(
                    PP[0][:, 512:1024].bitcast(BF16)[:, 128 * j:128 * (j + 1)],
                    q_tok[:, 128 * j:128 * (j + 1)], t_id[:, :])
            return last
        p.op("tensor", q_tp, reads=[gb("q"), gb("id")], writes=[pb[1]])

        def k_tp(t):
            last = None
            for j in range(8):
                last = t.transpose(
                    PP[0][:, 0:512].bitcast(BF16)[:, 128 * j:128 * (j + 1)],
                    kt[:, 128 * j:128 * (j + 1)], t_id[:, :])
            return last
        p.op("tensor", k_tp, reads=[gb("kt"), gb("id")], writes=[pb[0]])
        p.op("scalar", lambda s_: s_.copy(out=kT[:, :],
             in_=PP[0][:, 0:512].bitcast(BF16)), reads=[pb[0]], writes=[gb("kT")])
        p.op("scalar", lambda s_: s_.copy(out=qT[:, :],
             in_=PP[0][:, 512:1024].bitcast(BF16)), reads=[pb[1]],
             writes=[gb("qT")])
        rmm_g(0, idx)
        scan_g(0, idx)
        fslot(want_u, n=2)

        lnTv = lnT_sl[idx % 2].rearrange("p (k t) -> p k t", k=8)
        blnT = gb(f"lnT{idx % 2}")

        def pass2(t, lnTv=lnTv):
            last = None
            for k in range(8):
                st, sp_ = (k == 0), (k == 7)
                t.matmul(PP[0][:, 0:512], lnTv[:, k, :], WT["v"][:, k, 0:512],
                         start=st, stop=sp_)
                t.matmul(PP[0][:, 512:1024], lnTv[:, k, :],
                         WT["v"][:, k, 512:1024], start=st, stop=sp_)
                t.matmul(PP[1][:, 0:512], lnTv[:, k, :], WT["og"][:, k, 0:512],
                         start=st, stop=sp_)
                last = t.matmul(PP[1][:, 512:1024], lnTv[:, k, :],
                                WT["og"][:, k, 512:1024], start=st, stop=sp_)
            return last
        p.op("tensor", pass2, reads=[blnT, bw["v"], bw["og"]],
             writes=[pb[0], pb[1], pb[2], pb[3]], tag="pass2" + sfx)
        p.op("scalar", lambda s_: s_.activation(out=v_h[:, :], in_=PP[0][:, :],
             func=AF.Identity, bias=0.0, scale=0.5),
             reads=[pb[0], pb[1], gb("v")], writes=[gb("v")])
        p.op("scalar", lambda s_: s_.activation(out=thog[:, :], in_=PP[1][:, :],
             func=AF.Tanh, bias=0.0, scale=0.5), reads=[pb[2], pb[3]],
             writes=[gb("thog")])
        if pre is not None:
            pre(1)

        p.op("gpsimd", lambda g: g.tensor_tensor(
             out=k2[:, :].rearrange("p (h d) -> p h d", h=16),
             in0=kt[:, :].rearrange("p (h d) -> p h d", h=16),
             in1=E2[:, :].unsqueeze(2).broadcast_to([128, 16, 64]),
             op=ALU.mult), reads=[gb("kt"), gb("E2")], writes=[gb("k2")],
             tag="k2" + sfx)


        for g in range(2):
            ppR = PP[2 + g]
            pbR = [pb[4 + 2 * g], pb[5 + 2 * g]]

            def smm(t, g=g, ppR=ppR):
                last = None
                for j in range(8):
                    b0 = 64 * g
                    last = t.matmul(ppR[:, 128 * j:128 * (j + 1)],
                                    kT[b0:b0 + 64, 128 * j:128 * (j + 1)],
                                    qT[b0:b0 + 64, 128 * j:128 * (j + 1)],
                                    start=True, stop=True, tile_position=(b0, 0))
                return last
            p.op("tensor", smm, reads=[gb("kT"), gb("qT")], writes=pbR,
                 tag=f"smm{g}" + sfx)

        for g in range(2):
            ppR = PP[2 + g]
            pbR = [pb[4 + 2 * g], pb[5 + 2 * g]]
            dt_g = Dt if g == 0 else At
            bdt = gb("Dt") if g == 0 else gb("At")
            p.op("vector", lambda v, ppR=ppR, dt_g=dt_g: v.tensor_tensor(
                 out=dt_g[:, :], in0=ppR[:, :], in1=dt_g[:, :], op=ALU.mult),
                 reads=pbR, writes=[bdt], tag=f"att{g}" + sfx)
        if pre is not None:
            pre(2)

        def ointer(t):
            last = None
            for jj in range(8):
                for b in range(2):
                    pos = 8 * b + jj
                    last = t.matmul(PP[1][:, 64 * pos:64 * (pos + 1)],
                                    qT[64 * b:64 * b + 64,
                                       128 * jj:128 * (jj + 1)],
                                    S0bf[64 * b:64 * b + 64,
                                         512 * s + 64 * jj:512 * s + 64 * (jj + 1)],
                                    start=True, stop=True,
                                    tile_position=(64 * b, 0))
            return last
        p.op("tensor", ointer, reads=[gb("qT"), gb("S0bf")],
             writes=[pb[2], pb[3]], tag="ointer" + sfx)

        def oint(t):
            last = None
            for h in range(H):
                g = h % 2
                j = h // 2
                src = Dt if g == 0 else At
                last = t.matmul(PP[0][:, 64 * h:64 * (h + 1)],
                                src[:, 128 * j:128 * (j + 1)],
                                v_h[:, 64 * h:64 * (h + 1)],
                                start=True, stop=True)
            return last
        p.op("tensor", oint, reads=[gb("Dt"), gb("At"), gb("v")],
             writes=[pb[0], pb[1]], tag="oint" + sfx)

        def updmm(t):
            last = None
            for jj in range(8):
                for b in range(2):
                    h = 2 * jj + b
                    last = t.matmul(PP[2][64 * b:64 * b + 64,
                                          64 * jj:64 * (jj + 1)],
                                    k2[:, 64 * h:64 * (h + 1)],
                                    v_h[:, 64 * h:64 * (h + 1)],
                                    start=True, stop=True,
                                    tile_position=(0, 64 * b))
            return last
        p.op("tensor", updmm, reads=[gb("k2"), gb("v")], writes=[pb[4]],
             tag="upd" + sfx)

        def s0upd(v):
            last = None
            for jj in range(8):
                last = v.scalar_tensor_tensor(
                    out=S0[:, 512 * s + 64 * jj:512 * s + 64 * (jj + 1)],
                    in0=S0[:, 512 * s + 64 * jj:512 * s + 64 * (jj + 1)],
                    scalar=expGCp[:, jj:jj + 1],
                    in1=PP[2][:, 64 * jj:64 * (jj + 1)],
                    op0=ALU.mult, op1=ALU.add)
            return last
        p.op("vector", s0upd, reads=[pb[4], gb("expGCp"), gb("S0")],
             writes=[gb("S0")], tag="s0upd" + sfx)
        p.op("scalar", lambda s_: s_.copy(out=S0bf[:, 512 * s:512 * (s + 1)],
             in_=S0[:, 512 * s:512 * (s + 1)]), reads=[gb("S0")],
             writes=[gb("S0bf")])

        expG_b = (expG[:, :].rearrange("p (j b) -> p b j", b=2)
                  .unsqueeze(3).broadcast_to([128, 2, 8, 64]))
        p.op("vector", lambda v: v.tensor_tensor(
             out=tmpO[:, :].rearrange("p (b j d) -> p b j d", b=2, j=8),
             in0=PP[1][:, :].rearrange("p (b j d) -> p b j d", b=2, j=8),
             in1=expG_b, op=ALU.mult),
             reads=[pb[2], pb[3], gb("expG")], writes=[gb("tmpO")],
             tag="tt1" + sfx)
        p.op("vector", lambda v: v.tensor_tensor(
             out=attn[:, :].rearrange("p (j b d) -> p j b d", b=2, d=64),
             in0=tmpO[:, :].rearrange("p (b j d) -> p j b d", b=2, j=8),
             in1=PP[0][:, :].rearrange("p (j b d) -> p j b d", b=2, d=64),
             op=ALU.add),
             reads=[gb("tmpO"), pb[0], pb[1]], writes=[gb("attn")],
             tag="tt2" + sfx)
        sm1 = SMS[1]
        bns1 = sm1[:, 6:18]
        p.op("vector", lambda v: v.bn_stats(out=bns1[:, 0:6], in_=attn[:, 0:512]),
             reads=[gb("attn")], writes=[gb("bns1")])
        p.op("vector", lambda v: v.bn_stats(out=bns1[:, 6:12],
             in_=attn[:, 512:1024]), reads=[gb("attn")], writes=[gb("bns1")])
        p.op("vector", lambda v: v.bn_aggr(out=sm1[:, 4:6], in_=bns1[:, :]),
             reads=[gb("bns1")], writes=[gb("mv1")])
        p.op("vector", lambda v: v.tensor_scalar(out=sm1[:, 3:4],
             in0=sm1[:, 4:5], scalar1=sm1[:, 4:5], scalar2=sm1[:, 5:6],
             op0=ALU.mult, op1=ALU.add), reads=[gb("mv1")], writes=[gb("ms1")])
        if handoff is not None:
            handoff(1)
        p.op("scalar", lambda s_: s_.activation(out=sm1[:, 2:3], in_=sm1[:, 3:4],
             func=AF.Ln, bias=RMS_EPS, scale=1.0), reads=[gb("ms1")],
             writes=[gb("lv1")])
        p.op("scalar", lambda s_: s_.activation(out=sm1[:, 0:1], in_=sm1[:, 2:3],
             func=AF.Exp, bias=0.0, scale=-0.5), reads=[gb("lv1")],
             writes=[gb("rstd1")])
        p.op("vector", lambda v: v.scalar_tensor_tensor(out=tmpO[:, :],
             in0=thog[:, :], scalar=1.0, in1=attn[:, :], op0=ALU.add,
             op1=ALU.mult), reads=[gb("thog"), gb("attn"), gb("tmpO")],
             writes=[gb("tmpO")], tag="ymul0" + sfx)
        if handoff is not None:
            handoff(2)

        p.op("vector", lambda v: v.tensor_scalar(out=y_bf[:, :],
             in0=tmpO[:, :], scalar1=sm1[:, 0:1], scalar2=None, op0=ALU.mult),
             reads=[gb("tmpO"), gb("rstd1"), gb("thxg")],
             writes=[gb("thxg")], tag="ymul" + sfx)
        if handoff is not None:
            handoff(3)

        def y_tp(t):
            last = None
            for k in range(8):
                last = t.transpose(
                    PP[3][:, 0:512].bitcast(BF16)[:, 128 * k:128 * (k + 1)],
                    y_bf[:, 128 * k:128 * (k + 1)], t_id[:, :])
            return last
        p.op("tensor", y_tp, reads=[gb("thxg"), gb("id"), gb("kt")],
             writes=[pb[6]])
        p.op("vector", lambda v: v.tensor_copy(out=yT[:, :],
             in_=PP[3][:, 0:512].bitcast(BF16)), reads=[pb[6]], writes=[gb("kt")])
        yTv = yT.rearrange("p (k t) -> p k t", k=8)

        for hf in range(2):
            def womm_h(t, hf=hf):
                last = None
                for k in range(8):
                    last = t.matmul(PP[3][:, 0:512], yTv[:, k, :],
                                    WOT[:, k, 512 * hf:512 * (hf + 1)],
                                    start=(k == 0), stop=(k == 7))
                return last
            p.op("tensor", womm_h, reads=[gb("kt"), bw["o"]], writes=[pb[6]],
                 tag=f"womm{hf}" + sfx)
            p.op("vector", lambda v, hf=hf: v.tensor_tensor(
                 out=x1_t[:, 512 * hf:512 * (hf + 1)],
                 in0=xi[:, 512 * hf:512 * (hf + 1)],
                 in1=PP[3][:, 0:512], op=ALU.add), reads=[bxi, pb[6]],
                 writes=[gb("x1")], tag=f"resid{hf}" + sfx)
        p.dma("sync", out=x1_s[row0:row0 + CH, :], in_=x1_t[:, :],
              reads=[gb("x1")], writes=[gb("x1s")], tag="x1out" + sfx)

        sm2 = SMS[2]
        bns2 = sm2[:, 6:18]
        p.op("vector", lambda v: v.bn_stats(out=bns2[:, 0:6], in_=x1_t[:, 0:512]),
             reads=[gb("x1")], writes=[gb("bns2")])
        p.op("vector", lambda v: v.bn_stats(out=bns2[:, 6:12],
             in_=x1_t[:, 512:1024]), reads=[gb("x1")], writes=[gb("bns2")])
        p.op("vector", lambda v: v.bn_aggr(out=sm2[:, 4:6], in_=bns2[:, :]),
             reads=[gb("bns2")], writes=[gb("mv2")])
        p.op("scalar", lambda s_: s_.activation(out=sm2[:, 2:3], in_=sm2[:, 5:6],
             func=AF.Ln, bias=LN_EPS, scale=1.0), reads=[gb("mv2")],
             writes=[gb("lv2")])
        p.op("scalar", lambda s_: s_.activation(out=sm2[:, 0:1], in_=sm2[:, 2:3],
             func=AF.Exp, bias=0.0, scale=-0.5), reads=[gb("lv2")],
             writes=[gb("rstd2")])
        p.op("vector", lambda v: v.tensor_scalar(out=sm2[:, 1:2],
             in0=sm2[:, 4:5], scalar1=sm2[:, 0:1], scalar2=-1.0, op0=ALU.mult,
             op1=ALU.mult), reads=[gb("mv2"), gb("rstd2")], writes=[gb("negmr2")])
        p.op("vector", lambda v: v.tensor_scalar(out=ln2[:, :], in0=x1_t[:, :],
             scalar1=sm2[:, 0:1], scalar2=sm2[:, 1:2], op0=ALU.mult,
             op1=ALU.add),
             reads=[gb("x1"), gb("rstd2"), gb("negmr2"), gb("v")],
             writes=[gb("v")])
        p.op("vector", lambda v, idx=idx: v.tensor_reduce(
             out=am64[:, idx:idx + 1], in_=ln2[:, :], axis=AX.X, op=ALU.max,
             apply_absolute_value=True), reads=[gb("v")], writes=[gb("am64")])

        def ln2_tp(t):
            last = None
            for k in range(8):
                last = t.transpose(
                    PP[3][:, 0:512].bitcast(BF16)[:, 128 * k:128 * (k + 1)],
                    ln2[:, 128 * k:128 * (k + 1)], t_id[:, :])
            return last
        p.op("tensor", ln2_tp, reads=[gb("v"), gb("id")], writes=[pb[6]])
        p.op("vector", lambda v: v.tensor_copy(out=lnT2[:, :],
             in_=PP[3][:, 0:512].bitcast(BF16)), reads=[pb[6]],
             writes=[gb("lnT2b")])
        tok0 = s * S + c * CH
        p.dma("scalar",
              out=ln2t_s.rearrange("k p t -> p k t")[:, :, tok0:tok0 + CH],
              in_=lnT2.rearrange("p (k t) -> p k t", k=8),
              reads=[gb("lnT2b")], writes=[gb("ln2ts")], tag="ln2out" + sfx)
        if post is not None:
            post()

    order = [(s, c) for c in range(NC_) for s in range(n_seq)]
    for i in range(min(2, len(order))):
        s0, c0 = order[i]
        r0 = s0 * S + c0 * CH
        p.dma("sync", out=x_t[i % 2][:, :], in_=x_in[r0:r0 + CH, :],
              writes=[bx[i % 2]])

    ln1_stats(*order[0], 0)
    ln1_apply(*order[0], 0)
    emit_lnT(0)
    emit_ld(0)
    ldg_front(0)
    emit_qkg(0)
    ldg_back(0)
    expgcp_emit(0)
    rmm_g(1, 0)
    scan_g(1, 0)
    bw["v"] = prep_weight("v", Wv, o_wt["v"])
    bw["og"] = prep_weight("og", Wog, o_wt["og"])
    bw["o"] = prep_weight("o", Wo, o_wot, scale_col=True)
    emit_stfence()
    emit_fdma()
    for idx, (s0, c0) in enumerate(order):
        def pre(phase, idx=idx):
            if idx + 1 < len(order):
                if phase == 1:
                    ln1_stats(*order[idx + 1], idx + 1)
                else:
                    ln1_apply(*order[idx + 1], idx + 1)

        def handoff(phase, idx=idx):
            nxt = idx + 1 < len(order)
            if phase == 1:
                if nxt:
                    emit_lnT(idx + 1)
                return
            if phase == 2:
                lim = acb_ct[0] if idx >= len(order) - 2 else acb_ct[0] - 1
                if back_ct[0] < lim:
                    emit_back()
                if nxt:
                    emit_ld(idx + 1)
                while back_ct[0] < lim:
                    emit_back()
                if nxt:
                    emit_qkg(idx + 1)
                return
            if nxt:
                ldg_front(idx + 1)
            fslot((idx + 1) * NU // NTOT + 1)
            if nxt:
                ldg_back(idx + 1)

        def post(idx=idx):
            if idx + 1 < len(order):
                expgcp_emit(idx + 1)
                rmm_g(1, idx + 1)
                scan_g(1, idx + 1)
        chunk(s0, c0, idx, handoff=handoff, pre=pre, post=post)
        if idx + 2 < len(order):
            s2, c2 = order[idx + 2]
            r2 = s2 * S + c2 * CH
            p.dma("sync", out=x_t[(idx + 2) % 2][:, :], in_=x_in[r2:r2 + CH, :],
                  writes=[bx[(idx + 2) % 2]], tag=f"xld{idx + 2}")
    while back_ct[0] < NU:
        emit_fdma()
        emit_fa()
        emit_fb()
        if back_ct[0] < acb_ct[0]:
            emit_back()

    if stage <= 1:
        p.build()
        return nc

    def cross_part_max2(src, bsrcs, cc_i, width, tagn):
        p.op("tensor", lambda t: t.transpose(PP[3][0:width, 512:640], src,
             t_idf[:, :]), reads=list(bsrcs) + [gb("idf")], writes=[pb[7]],
             tag=tagn + "tp")
        p.op("vector", lambda v: v.tensor_reduce(
             out=AFA[0:width, o_sc + 62:o_sc + 63],
             in_=PP[3][0:width, 512:640], axis=AX.X, op=ALU.max),
             reads=[], writes=[pb[7], gb("ccv" + tagn)], tag=tagn + "red")
        p.dma("gpsimd", out=cc_i.ap()[0:1, 0:width],
              in_=AFA[0:width, o_sc + 62:o_sc + 63],
              reads=[gb("ccv" + tagn)], writes=[gb("ccin" + tagn)],
              tag=tagn + "st")

    def allreduce(i, width, tagn):
        def cc(g, i=i):
            return g.collective_compute(
                "AllReduce", ALU.max, replica_groups=[list(range(n_cores))],
                ins=[cc_in[i].ap()[0:1, 0:width].opt()],
                outs=[cc_out[i].ap()[0:1, 0:width].opt()])
        p.op("gpsimd", cc, reads=[gb("ccin" + tagn)], writes=[gb("ccout" + tagn)],
             tag=tagn + "ar")
        dst = AFA[:, o_sc + 8 * i:o_sc + 8 * i + width]
        p.dma("gpsimd", out=dst,
              in_=cc_out[i].ap()[0:1, 0:width].partition_broadcast(128).squeeze(1),
              reads=[gb("ccout" + tagn)], writes=[gb("gmax" + tagn)],
              tag=tagn + "rd")
        return dst

    amfin = AFA[:, o_am + 64:o_am + 65]
    p.op("vector", lambda v: v.tensor_reduce(out=amfin, in_=am64[:, :],
         axis=AX.X, op=ALU.max), reads=[gb("am64")], writes=[gb("amfin")])
    cross_part_max2(amfin, [gb("amfin")], cc_in[0], 1, "ar1")
    g1 = allreduce(0, 1, "ar1")
    s1 = AFA[:, o_sc + 32:o_sc + 33]
    r1 = AFA[:, o_sc + 33:o_sc + 34]
    p.op("vector", lambda v: v.tensor_scalar(out=s1, in0=g1, scalar1=1.0 / QMAX,
         scalar2=1e-12, op0=ALU.mult, op1=ALU.max), reads=[gb("gmaxar1")],
         writes=[gb("s1")])
    p.op("vector", lambda v: v.reciprocal(out=r1, in_=s1), reads=[gb("s1")],
         writes=[gb("r1")])

    xq = abv(o_wt["q"], 8 * NT)
    bxq = gb("xq")
    old_wbufs = [bw["q"], bw["k"], bw["v"], bw["g"]]
    p.dma("sync", out=xq.rearrange("p (k t) -> p k t", k=8),
          in_=ln2t_s.rearrange("k p t -> p k t"),
          reads=[gb("ln2ts")], writes=old_wbufs + [bxq], tag="xqld")
    i16tmp = abv(o_stage, 4096, I16)
    p.op("vector", lambda v: v.memset(AFA[0:1, o_sc + 1:o_sc + 2], 0.0),
         reads=[gb("wqi0"), gb("wqi1"), gb("wdq0"), gb("wdq1"), gb("wdq2"),
                gb("wev0"), gb("wev1"),
                gb("stage0"), gb("lnT1")],
         writes=[gb("i16t"), gb("i16u")], tag="b1fence")
    xqv = xq.rearrange("p (k t) -> p k t", k=8)
    i16tmp2 = abv(o_stage + 6144, 2048, I16)
    for k in range(8):
        for off in range(0, NT, 2048):
            w = min(2048, NT - off)
            sl = xqv[:, k, off:off + w]
            if k % 2 == 0:
                p.op("vector", lambda v, sl=sl, w=w: v.tensor_scalar(
                     out=i16tmp[:, 0:w], in0=sl, scalar1=r1, scalar2=None,
                     op0=ALU.mult), reads=[bxq, gb("r1")], writes=[gb("i16t")],
                     tag=f"xqr{k}{off}")
                p.op("vector", lambda v, sl=sl, w=w: v.tensor_copy(
                     out=sl, in_=i16tmp[:, 0:w]), reads=[gb("i16t")],
                     writes=[bxq], tag=f"xqc{k}{off}")
            else:
                p.op("scalar", lambda s_, sl=sl, w=w: s_.activation(
                     out=i16tmp2[:, 0:w], in_=sl, func=AF.Identity, bias=0.0,
                     scale=r1), reads=[bxq, gb("r1")], writes=[gb("i16u")],
                     tag=f"xqr{k}{off}")
                p.op("scalar", lambda s_, sl=sl, w=w: s_.copy(
                     out=sl, in_=i16tmp2[:, 0:w]), reads=[gb("i16u")],
                     writes=[bxq], tag=f"xqc{k}{off}")

    if stage <= 2:
        p.build()
        return nc

    wtc = [abv(o_stage + 1024 * j, 1024) for j in range(4)]
    bwtc = [gb(f"wtc{j}") for j in range(4)]
    p.op("vector", lambda v: v.memset(AFA[0:1, o_sc + 6:o_sc + 7], 0.0),
         reads=[gb("i16t")], writes=[gb("wtc0"), gb("wtc1"), gb("wtc2"),
                                     gb("wtc3")], tag="b1fence2")
    b1s = [abv(OB1S + NT * j, NT) for j in range(4)]
    bb1s = [gb(f"b1s{j}") for j in range(4)]
    p.op("vector", lambda v: v.memset(AFA[0:1, o_sc + 2:o_sc + 3], 0.0),
         reads=[bw["og"], bw["o"]], writes=bb1s, tag="b1sfence")
    amU24 = am64[:, 0:24]
    amG24 = am64[:, 24:48]
    upt_v = upt_s.rearrange("i p t -> p i t")
    gat_v = gat_s.rearrange("i p t -> p i t")

    def b1_block(i, wsrc_s, bsrc, slot, dst_v, bdst, amcol, tagn):
        wv = wtc[slot].rearrange("p (k o) -> p k o", k=8)
        p.dma("sync", out=wtc[slot][:, :], in_=wsrc_s[i], reads=[bsrc],
              writes=[bwtc[slot]], tag=f"b1w{tagn}{i}", sem_group=f"wtc{slot}")
        strip = b1s[slot]
        bstrip = bb1s[slot]
        ngrp = max(1, TT1 // 4)
        for tg in range(ngrp):
            nt4 = min(4, TT1 - 4 * tg)
            ppA = 2 * (tg % 2)
            pbs = [pb[2 * ppA + j] for j in range(2 * nt4 // 2 + (nt4 % 2))]
            pbs = [pb[2 * ppA + j] for j in range(nt4)]

            def mm(t, wv=wv, tg=tg, nt4=nt4, ppA=ppA):
                last = None
                for k in range(8):
                    for t4 in range(nt4):
                        tt = 4 * tg + t4
                        pt = PP[ppA + t4 // 2][:, 512 * (t4 % 2):
                                               512 * (t4 % 2) + 512]
                        last = t.matmul(pt, wv[:, k, :],
                                        xqv[:, k, 512 * tt:512 * (tt + 1)],
                                        start=(k == 0), stop=(k == 7))
                return last
            p.op("tensor", mm, reads=[bwtc[slot], bxq], writes=pbs,
                 tag=f"b1{tagn}mm{i}_{tg}")
            for t4 in range(nt4):
                tt = 4 * tg + t4
                pt = PP[ppA + t4 // 2][:, 512 * (t4 % 2):512 * (t4 % 2) + 512]
                p.op("scalar", lambda s_, pt=pt, tt=tt, strip=strip:
                     s_.activation(out=strip[:, 512 * tt:512 * (tt + 1)],
                                   in_=pt, func=AF.Identity, bias=0.0,
                                   scale=s1),
                     reads=[pb[2 * ppA + t4], gb("s1")], writes=[bstrip],
                     tag=f"b1{tagn}ev{i}_{tt}")
        p.op("vector", lambda v, strip=strip, amcol=amcol: v.tensor_reduce(
             out=amcol, in_=strip[:, :], axis=AX.X, op=ALU.max,
             apply_absolute_value=True), reads=[bstrip], writes=[gb("am64")],
             tag=f"b1{tagn}am{i}")
        p.dma("scalar", out=dst_v[:, i, :], in_=strip[:, :], reads=[bstrip],
              writes=[bdst], tag=f"b1{tagn}st{i}")

    for i in range(NIC):
        b1_block(i, wut_s, gb("wuts"), i % 2, upt_v, gb("upts"),
                 amU24[:, i:i + 1], "u")
        b1_block(i, wgt_s, gb("wgts"), 2 + i % 2, gat_v, gb("gats"),
                 amG24[:, i:i + 1], "g")

    if stage <= 3:
        p.build()
        return nc

    amUG = AFA[:, o_sc + 40:o_sc + 42]
    p.op("vector", lambda v: v.tensor_reduce(out=amUG[:, 0:1], in_=amU24,
         axis=AX.X, op=ALU.max), reads=[gb("am64")], writes=[gb("amUf")])
    p.op("vector", lambda v: v.tensor_reduce(out=amUG[:, 1:2], in_=amG24,
         axis=AX.X, op=ALU.max), reads=[gb("am64")], writes=[gb("amGf")])
    cross_part_max2(amUG, [gb("amUf"), gb("amGf")], cc_in[1], 2, "ar2")
    g2 = allreduce(1, 2, "ar2")
    s2 = AFA[:, o_sc + 34:o_sc + 35]
    r2 = AFA[:, o_sc + 35:o_sc + 36]
    sg = AFA[:, o_sc + 36:o_sc + 37]
    csig = AFA[:, o_sc + 37:o_sc + 38]
    rg = AFA[:, o_sc + 49:o_sc + 50]
    p.op("vector", lambda v: v.tensor_scalar(out=s2, in0=g2[:, 0:1],
         scalar1=1.0 / QMAX, scalar2=1e-12, op0=ALU.mult, op1=ALU.max),
         reads=[gb("gmaxar2")], writes=[gb("s2")])
    p.op("vector", lambda v: v.reciprocal(out=r2, in_=s2), reads=[gb("s2")],
         writes=[gb("r2")])
    p.op("vector", lambda v: v.tensor_scalar(out=sg, in0=g2[:, 1:2],
         scalar1=1.0 / QMAX, scalar2=1e-12, op0=ALU.mult, op1=ALU.max),
         reads=[gb("gmaxar2")], writes=[gb("sg")])
    p.op("vector", lambda v: v.tensor_scalar(out=csig, in0=sg, scalar1=0.5,
         scalar2=None, op0=ALU.mult), reads=[gb("sg")], writes=[gb("csig")])
    p.op("vector", lambda v: v.reciprocal(out=rg, in_=sg), reads=[gb("sg")],
         writes=[gb("rg")])

    WDN = abv(OB_WDN, 24 * 1024).rearrange("p (i r) -> p i r", i=24)
    bwdn = gb("wdnsb")
    gs = [abv(OB2S + NT * j, NT) for j in range(10)]
    bgs = [gb(f"gs{j}") for j in range(10)]
    p.op("vector", lambda v: v.memset(AFA[0:1, o_sc + 3:o_sc + 4], 0.0),
         reads=[bw["og"], bw["o"], gb("wdt"), gb("i16t"), gb("i16u"),
                gb("wdq0"), gb("wdq1"), gb("wdq2"), gb("wev0"), gb("wev1"),
                gb("wtc0"), gb("wtc1"), gb("wtc2"), gb("wtc3"),
                gb("b1s0"), gb("b1s1"), gb("b1s2"), gb("b1s3"),
                gb("ln"), gb("lnT0"), gb("lnT1"), gb("q"), gb("thxg"),
                gb("kt"), gb("v"),
                gb("thog"), gb("kT"), gb("qT"), gb("Dt"), gb("At"),
                gb("k2"), gb("lnT2b"), bxq],
         writes=[bwdn] + bgs, tag="wdnfence")
    p.dma("sync", out=WDN, in_=wdnt_s.rearrange("i p d -> p i d"),
          reads=[gb("wdns")], writes=[bwdn], tag="wdnld")

    amT_cols = am64[:, 0:24]
    amO_cols = am64[:, 24:48]

    def b2_head(i):
        gl, bgl = gs[i % 2], bgs[i % 2]
        gi = gs[2 + i % 2].bitcast(I16)
        bgi = bgs[2 + i % 2]
        th = gs[4 + i % 2]
        bth = bgs[4 + i % 2]
        sig = gs[6 + i % 2].bitcast(U16)
        bsig = bgs[6 + i % 2]
        p.dma("sync", out=gl[:, :], in_=gat_v[:, i, :], reads=[gb("gats")],
              writes=[bgl], tag=f"b2ld{i}", sem_group=f"b2g{i % 2}")
        p.op("scalar", lambda s_, gl=gl, gi=gi: s_.activation(out=gi[:, :],
             in_=gl[:, :], func=AF.Identity, bias=0.0, scale=rg),
             reads=[bgl, gb("rg")], writes=[bgi], tag=f"b2gi{i}")
        p.op("scalar", lambda s_, gi=gi, th=th: s_.activation(out=th[:, :],
             in_=gi[:, :], func=AF.Tanh, bias=0.0, scale=csig),
             reads=[bgi, gb("csig")], writes=[bth], tag=f"b2th{i}")
        p.op("gpsimd", lambda g, th=th, sig=sig: g.tensor_scalar(out=sig[:, :],
             in0=th[:, :], scalar1=32767.5, scalar2=32768.0, op0=ALU.mult,
             op1=ALU.add), reads=[bth], writes=[bsig], tag=f"b2sig{i}")

    def b2_tail(i):
        gi = gs[2 + i % 2].bitcast(I16)
        bgi = bgs[2 + i % 2]
        sig = gs[6 + i % 2].bitcast(U16)
        bsig = bgs[6 + i % 2]
        tout = gs[8 + i % 2]
        bto = bgs[8 + i % 2]
        p.op("vector", lambda v, gi=gi, sig=sig, tout=tout: v.tensor_tensor(
             out=tout[:, :], in0=gi[:, :], in1=sig[:, :], op=ALU.mult),
             reads=[bgi, bsig], writes=[bto], tag=f"b2t{i}")
        p.op("vector", lambda v, tout=tout, i=i: v.tensor_reduce(
             out=amT_cols[:, i:i + 1], in_=tout[:, :], axis=AX.X, op=ALU.max,
             apply_absolute_value=True), reads=[bto], writes=[gb("am64")],
             tag=f"b2am{i}")
        p.dma("scalar", out=tra_s.rearrange("i p t -> p i t")[:, i, :],
              in_=tout[:, :], reads=[bto], writes=[gb("tras")], tag=f"b2st{i}")

    for i in range(NIC):
        b2_head(i)
        if i >= 1:
            b2_tail(i - 1)
    b2_tail(NIC - 1)

    if stage <= 4:
        p.build()
        return nc

    amT = AFA[:, o_sc + 43:o_sc + 44]
    p.op("vector", lambda v: v.tensor_reduce(out=amT, in_=amT_cols, axis=AX.X,
         op=ALU.max), reads=[gb("am64")], writes=[gb("amTf")])
    cross_part_max2(amT, [gb("amTf")], cc_in[2], 1, "ar3")
    g3 = allreduce(2, 1, "ar3")
    s3 = AFA[:, o_sc + 38:o_sc + 39]
    c3 = AFA[:, o_sc + 39:o_sc + 40]
    tmp1 = AFA[:, o_sc + 44:o_sc + 45]
    p.op("vector", lambda v: v.tensor_scalar(out=tmp1, in0=g3, scalar1=sg,
         scalar2=1.0 / 65536.0 / QMAX, op0=ALU.mult, op1=ALU.mult),
         reads=[gb("gmaxar3"), gb("sg")], writes=[gb("tmp1")])
    p.op("vector", lambda v: v.tensor_scalar(out=s3, in0=tmp1, scalar1=1e-12,
         scalar2=None, op0=ALU.max), reads=[gb("tmp1")], writes=[gb("s3")])
    p.op("vector", lambda v: v.reciprocal(out=c3, in_=s3), reads=[gb("s3")],
         writes=[gb("c3")])
    p.op("vector", lambda v: v.tensor_scalar(out=c3, in0=c3, scalar1=sg,
         scalar2=1.0 / 65536.0, op0=ALU.mult, op1=ALU.mult),
         reads=[gb("c3"), gb("sg")], writes=[gb("c3")])

    def b3_head(i):
        tl, btl = gs[i % 2], bgs[i % 2]
        ul, bul = gs[2 + i % 2], bgs[2 + i % 2]
        g2i = gs[4 + i % 2].bitcast(I16)
        bg2 = bgs[4 + i % 2]
        ui = gs[6 + i % 2].bitcast(I16)
        bui = bgs[6 + i % 2]
        p.dma("sync", out=tl[:, :],
              in_=tra_s.rearrange("i p t -> p i t")[:, i, :],
              reads=[gb("tras")], writes=[btl], tag=f"b3ldt{i}",
              sem_group=f"b3t{i % 2}")
        p.dma("gpsimd", out=ul[:, :], in_=upt_v[:, i, :], reads=[gb("upts")],
              writes=[bul], tag=f"b3ldu{i}", sem_group=f"b3u{i % 2}")
        p.op("scalar", lambda s_, tl=tl, g2i=g2i: s_.activation(
             out=g2i[:, :], in_=tl[:, :], func=AF.Identity, bias=0.0,
             scale=c3), reads=[btl, gb("c3")], writes=[bg2], tag=f"b3g2{i}")
        p.op("scalar", lambda s_, ul=ul, ui=ui: s_.activation(
             out=ui[:, :], in_=ul[:, :], func=AF.Identity, bias=0.0,
             scale=r2), reads=[bul, gb("r2")], writes=[bui], tag=f"b3ui{i}")

    def b3_tail(i):
        g2i = gs[4 + i % 2].bitcast(I16)
        bg2 = bgs[4 + i % 2]
        ui = gs[6 + i % 2].bitcast(I16)
        bui = bgs[6 + i % 2]
        oout = gs[8 + i % 2]
        bo = bgs[8 + i % 2]
        p.op("vector", lambda v, g2i=g2i, ui=ui, oout=oout: v.tensor_tensor(
             out=oout[:, :], in0=g2i[:, :], in1=ui[:, :], op=ALU.mult),
             reads=[bg2, bui], writes=[bo], tag=f"b3o{i}")
        p.op("vector", lambda v, oout=oout, i=i: v.tensor_reduce(
             out=amO_cols[:, i:i + 1], in_=oout[:, :], axis=AX.X, op=ALU.max,
             apply_absolute_value=True), reads=[bo], writes=[gb("am64")],
             tag=f"b3am{i}")
        p.dma("scalar", out=ora_s.rearrange("i p t -> p i t")[:, i, :],
              in_=oout[:, :], reads=[bo], writes=[gb("oras")], tag=f"b3st{i}")

    for i in range(NIC):
        b3_head(i)
        if i >= 1:
            b3_tail(i - 1)
    b3_tail(NIC - 1)

    if stage <= 5:
        p.build()
        return nc

    amO = AFA[:, o_sc + 45:o_sc + 46]
    p.op("vector", lambda v: v.tensor_reduce(out=amO, in_=amO_cols, axis=AX.X,
         op=ALU.max), reads=[gb("am64")], writes=[gb("amOf")])
    cross_part_max2(amO, [gb("amOf")], cc_in[3], 1, "ar4")
    g4 = allreduce(3, 1, "ar4")
    s4 = AFA[:, o_sc + 46:o_sc + 47]
    c4 = AFA[:, o_sc + 47:o_sc + 48]
    s32 = AFA[:, o_sc + 48:o_sc + 49]
    p.op("vector", lambda v: v.tensor_scalar(out=s32, in0=s3, scalar1=s2,
         scalar2=None, op0=ALU.mult), reads=[gb("s3"), gb("s2")],
         writes=[gb("s32")])
    p.op("vector", lambda v: v.tensor_scalar(out=s4, in0=g4, scalar1=s32,
         scalar2=1.0 / QMAX, op0=ALU.mult, op1=ALU.mult),
         reads=[gb("gmaxar4"), gb("s32")], writes=[gb("s4")])
    p.op("vector", lambda v: v.tensor_scalar(out=s4, in0=s4, scalar1=1e-12,
         scalar2=None, op0=ALU.max), reads=[gb("s4")], writes=[gb("s4")])
    p.op("vector", lambda v: v.reciprocal(out=c4, in_=s4), reads=[gb("s4")],
         writes=[gb("c4")])
    p.op("vector", lambda v: v.tensor_scalar(out=c4, in0=c4, scalar1=s32,
         scalar2=None, op0=ALU.mult), reads=[gb("c4"), gb("s32")],
         writes=[gb("c4")])

    NG = NT // 512
    xo2 = [abv(OB_XO + 12288 * j, 24 * 512) for j in range(2)]
    bxo2 = [gb("xo0"), gb("xo1")]
    x1l = [afv(o_x[0], 1024), afv(o_x[1], 1024)]
    bx1l = [gb("x1l0"), gb("x1l1")]
    outt = [afv(o_attn, 1024), afv(o_tmpO, 1024)]
    bout = [gb("bo0"), gb("bo1")]
    p.op("vector", lambda v: v.memset(AFA[0:1, o_sc + 5:o_sc + 6], 0.0),
         reads=[gb("gs2"), gb("gs3"), gb("gs4"), gb("gs5"), gb("gs6"),
                gb("gs7"), gb("gs8"), gb("gs9")],
         writes=[bxo2[0], bxo2[1], gb("attn"), gb("tmpO"), gb("x1"),
                 bx[0], bx[1], bout[0], bout[1], bx1l[0], bx1l[1]],
         tag="xofence")

    def b4_conv(gi_):
        t0 = 512 * gi_
        xov = xo2[gi_ % 2].rearrange("p (i t) -> p i t", i=24)
        bxo = bxo2[gi_ % 2]
        for i in range(NIC):
            sl4 = (0, 1, 8, 9)[i % 4]
            ol, bol = gs[sl4], bgs[sl4]
            p.dma("gpsimd", out=ol[:, 0:512],
                  in_=ora_s.rearrange("i p t -> p i t")[:, i, t0:t0 + 512],
                  reads=[gb("oras")], writes=[bol], tag=f"b4ld{gi_}_{i}",
                  sem_group=f"b4o{i % 4}")
            p.op("vector", lambda v, ol=ol: v.tensor_scalar(
                 out=ol[:, 0:512].bitcast(I16), in0=ol[:, 0:512], scalar1=c4,
                 scalar2=None, op0=ALU.mult), reads=[bol, gb("c4")],
                 writes=[bol], tag=f"b4xi{gi_}_{i}")
            p.op("vector", lambda v, ol=ol, i=i, xov=xov: v.tensor_copy(
                 out=xov[:, i, :], in_=ol[:, 0:512].bitcast(I16)),
                 reads=[bol], writes=[bxo], tag=f"b4xb{gi_}_{i}")

    def b4_mms(gi_):
        t0 = 512 * gi_
        xov = xo2[gi_ % 2].rearrange("p (i t) -> p i t", i=24)
        bxo = bxo2[gi_ % 2]
        for tp_ in range(2):
            wi = tp_ % 2
            pA = 2 * wi
            for h_ in range(2):
                tok = t0 + 128 * (2 * tp_ + h_)
                p.dma("sync", out=x1l[h_][:, :], in_=x1_s[tok:tok + CH, :],
                      reads=[gb("x1s")], writes=[bx1l[h_]],
                      tag=f"b4x1{gi_}_{tp_}_{h_}", sem_group=f"b4x1{h_}")
            pbd = [pb[2 * pA], pb[2 * pA + 1], pb[2 * pA + 2], pb[2 * pA + 3]]

            def dmm(t, tp_=tp_, pA=pA, xov=xov):
                last = None
                for i in range(NIC):
                    st, sp_ = (i == 0), (i == NIC - 1)
                    for h_ in range(2):
                        tt = 2 * tp_ + h_
                        t.matmul(PP[pA + h_][:, 0:512],
                                 xov[:, i, 128 * tt:128 * (tt + 1)],
                                 WDN[:, i, 0:512], start=st, stop=sp_)
                        last = t.matmul(PP[pA + h_][:, 512:1024],
                                        xov[:, i, 128 * tt:128 * (tt + 1)],
                                        WDN[:, i, 512:1024], start=st, stop=sp_)
                return last
            p.op("tensor", dmm, reads=[bxo, bwdn], writes=pbd,
                 tag=f"b4mm{gi_}_{tp_}")
            for h_ in range(2):
                tok = t0 + 128 * (2 * tp_ + h_)
                ppd = PP[pA + h_]
                p.op("vector", lambda v, h_=h_, ppd=ppd: v.scalar_tensor_tensor(
                     out=outt[h_][:, :], in0=ppd[:, :], scalar=s4,
                     in1=x1l[h_][:, :], op0=ALU.mult, op1=ALU.add),
                     reads=[pb[2 * (pA + h_)], pb[2 * (pA + h_) + 1],
                            bx1l[h_], gb("s4")], writes=[bout[h_]],
                     tag=f"b4ev{gi_}_{tp_}_{h_}")
                p.dma("sync", out=out_d[tok:tok + CH, :], in_=outt[h_][:, :],
                      reads=[bout[h_]], writes=[gb("outd")],
                      tag=f"b4st{gi_}_{tp_}_{h_}")

    for gi_ in range(NG):
        b4_conv(gi_)
        if gi_ >= 1:
            b4_mms(gi_ - 1)
    b4_mms(NG - 1)

    p.build()
    return nc


def _consts():
    idbf = np.eye(128, dtype=BF)
    idf32 = np.eye(128, dtype=np.float32)
    idt = np.tile(np.eye(128, dtype=np.float32), (1, 8)).astype(BF)
    sel = np.zeros((2, 16, 1024), dtype=BF)
    for g in range(2):
        for j in range(8):
            sel[g, 2 * j + g, 128 * j:128 * (j + 1)] = 1
    ones16 = np.ones((16, 128), dtype=np.float32)
    pm = np.zeros((16, 8), dtype=np.float32)
    for k in range(16):
        pm[k, k // 2] = 1
    eo = np.zeros((16, 128), dtype=np.float32)
    for k in range(16):
        for m in range(128):
            if (k % 2) == (1 if m >= 64 else 0):
                eo[k, m] = 1
    aconst = np.array([[1e-5, float(np.finfo(np.float32).eps), 1.0, -1.0,
                        0.5, -0.5, 0.0, 1.0 / 65536.0]], dtype=np.float32)
    return dict(idbf=idbf, idf32=idf32, idt=idt, sel=sel, ones16=ones16,
                pmask=pm, eomask=eo, aconst=aconst)


def kernel(L0_attn_x, L0_attn_Wq, L0_attn_Wk, L0_attn_Wv, L0_attn_Wo,
           L0_attn_Wg, L0_attn_Wog, L0_attn_Wd, L0_attn_bd, L0_attn_norm_w,
           L0_ffn_w_gate, L0_ffn_w_up, L0_ffn_w_down):
    x = np.ascontiguousarray(np.asarray(L0_attn_x, dtype=np.float32))
    Bt, S, _ = x.shape
    n_cores = 8
    per = Bt // n_cores
    nc = build_program(n_seq=per, S=S, n_cores=n_cores)
    consts = _consts()
    common = dict(
        Wq=np.ascontiguousarray(L0_attn_Wq, np.float32),
        Wk=np.ascontiguousarray(L0_attn_Wk, np.float32),
        Wv=np.ascontiguousarray(L0_attn_Wv, np.float32),
        Wo=np.ascontiguousarray(L0_attn_Wo, np.float32),
        Wg=np.ascontiguousarray(L0_attn_Wg, np.float32),
        Wog=np.ascontiguousarray(L0_attn_Wog, np.float32),
        Wd=np.ascontiguousarray(L0_attn_Wd, np.float32),
        bd=np.ascontiguousarray(np.asarray(L0_attn_bd, np.float32).reshape(16, 1)),
        nw=np.ascontiguousarray(np.asarray(L0_attn_norm_w, np.float32).reshape(1, -1)),
        w_gate=np.ascontiguousarray(L0_ffn_w_gate, np.float32),
        w_up=np.ascontiguousarray(L0_ffn_w_up, np.float32),
        w_down=np.ascontiguousarray(L0_ffn_w_down, np.float32),
        **consts,
    )
    in_maps = []
    for c in range(n_cores):
        xc = np.ascontiguousarray(x[c * per:(c + 1) * per].reshape(per * S, -1))
        in_maps.append(dict(x=xc, **common))
    res = run_bass_kernel_spmd(nc, in_maps, core_ids=list(range(n_cores)))
    outs = [np.asarray(res.results[c]["out"]).reshape(per, S, -1)
            for c in range(n_cores)]
    return np.concatenate(outs, axis=0)
